# version 15
# baseline (speedup 1.0000x reference)
import numpy as np
from contextlib import ExitStack

import concourse.bass as bass
import concourse.mybir as mybir
from concourse.bass_utils import run_bass_kernel_spmd
from concourse.bass_core_rust import IndirectOffsetOnAxis

F32 = mybir.dt.float32
BF16 = mybir.dt.bfloat16
I32 = mybir.dt.int32
AF = mybir.ActivationFunctionType
ALU = mybir.AluOpType
AX = mybir.AxisListType

D = 1024
N = 8192
KC = 8
H = 8
CH = 64
NT = 16
TT = 512
NST = 4
ST = 1024
DFF = 4096
NE = 8
CS = float(np.float32(0.01 * 2.0 / 128.0))
MAGIC = 0x5F3759DF

ATTN_ON = True


class Chan:
    def __init__(self, es, nc, name):
        self.sem = es.enter_context(nc.semaphore(name))
        self.n = 0

    def inc(self, inst, k=1):
        inst.then_inc(self.sem, k)
        self.n += k
        return self.n

    def wait(self, eng, val=None):
        eng.wait_ge(self.sem, self.n if val is None else val)


def _rsqrt(v, ch, out, s_ap, t1, t2, iters=2):
    def e(inst):
        ch.inc(inst)
        ch.wait(v)
    oi = out.bitcast(I32)
    si = s_ap.bitcast(I32)
    e(v.tensor_scalar(oi, si, 1, None, ALU.arith_shift_right))
    e(v.tensor_scalar(oi, oi, -1, MAGIC, ALU.mult, ALU.add))
    e(v.tensor_scalar(t2, s_ap, 0.5, None, ALU.mult))
    for _ in range(iters):
        e(v.tensor_mul(t1, out, out))
        e(v.tensor_mul(t1, t1, t2))
        e(v.tensor_scalar(t1, t1, -1.0, 1.5, ALU.mult, ALU.add))
        e(v.tensor_mul(out, out, t1))


def build_nc(attn=ATTN_ON, debug=False, p2stage=99, rcut=99, p1cut=99,
             ntile=NT, fcut=5, scut=99):
    nc = bass.Bass("TRN2", target_bir_lowering=False)

    d_seq = nc.dram_tensor("seq_fm", [D, N], F32, kind="ExternalInput").ap()
    d_xwin = nc.dram_tensor("xwin_fm", [D, 4096], F32, kind="ExternalInput").ap()
    d_gidx = nc.dram_tensor("gidx", [128, KC], I32, kind="ExternalInput").ap()
    d_wkv = nc.dram_tensor("wkv", [D, 2 * D], F32, kind="ExternalInput").ap()
    d_wq = nc.dram_tensor("wq", [D, D], F32, kind="ExternalInput").ap()
    d_wcomb = nc.dram_tensor("wcomb", [D, D], F32, kind="ExternalInput").ap()
    d_wsg = nc.dram_tensor("wsg", [D, 16], F32, kind="ExternalInput").ap()
    d_wmd = nc.dram_tensor("wmd", [D, 16], F32, kind="ExternalInput").ap()
    d_smg = nc.dram_tensor("smg", [128, KC], F32, kind="ExternalInput").ap()
    d_rmg = nc.dram_tensor("rmg", [128, KC], F32, kind="ExternalInput").ap()
    d_mw0 = nc.dram_tensor("mw0", [128, 128], F32, kind="ExternalInput").ap()
    d_mw1 = nc.dram_tensor("mw1", [128, 128], F32, kind="ExternalInput").ap()
    d_mhg = nc.dram_tensor("mhg", [128, H], F32, kind="ExternalInput").ap()
    d_rw = nc.dram_tensor("rw", [D, NE], F32, kind="ExternalInput").ap()
    d_rb = nc.dram_tensor("rb", [1, NE], F32, kind="ExternalInput").ap()
    d_w1 = nc.dram_tensor("w1e", [NE, D, DFF], F32, kind="ExternalInput").ap()
    d_w2 = nc.dram_tensor("w2e", [NE, DFF, D], F32, kind="ExternalInput").ap()
    d_b1r = nc.dram_tensor("b1r", [NE, DFF], F32, kind="ExternalInput").ap()
    d_b2r = nc.dram_tensor("b2r", [NE, D], F32, kind="ExternalInput").ap()
    d_tri = nc.dram_tensor("c_tri", [128, 128], F32, kind="ExternalInput").ap()
    d_tri8 = nc.dram_tensor("c_tri8", [8, 8], F32, kind="ExternalInput").ap()
    d_tokid = nc.dram_tensor("c_tokid", [128, 32], I32, kind="ExternalInput").ap()
    d_dcol = nc.dram_tensor("c_dcol", [128, 8], F32, kind="ExternalInput").ap()
    d_slot = nc.dram_tensor("c_slot", [8, 24], F32, kind="ExternalInput").ap()
    d_ones = nc.dram_tensor("c_ones", [128, 128], F32, kind="ExternalInput").ap()
    d_id = nc.dram_tensor("c_id", [128, 128], F32, kind="ExternalInput").ap()
    d_oh8 = nc.dram_tensor("c_oh8", [8, 8 * 128], F32, kind="ExternalInput").ap()
    d_ohc8 = nc.dram_tensor("c_ohc8", [128, 64], F32, kind="ExternalInput").ap()
    d_bd8 = nc.dram_tensor("c_bd8", [8, 64], F32, kind="ExternalInput").ap()

    d_out = nc.dram_tensor("out", [4096, D], F32, kind="ExternalOutput").ap()

    d_w1b = nc.dram_tensor("scr_w1b", [NE * D, DFF], BF16, kind="Internal").ap()
    d_w2b = nc.dram_tensor("scr_w2b", [NE * DFF, D], BF16, kind="Internal").ap()
    d_xtm = nc.dram_tensor("scr_xtm", [4224, D], BF16, kind="Internal").ap()
    d_ytm = nc.dram_tensor("scr_ytm", [4224, D], BF16, kind="Internal").ap()
    d_sid = nc.dram_tensor("scr_sid", [6144, 1], I32, kind="Internal").ap()
    d_gat = nc.dram_tensor("scr_gat", [4224, 1], F32, kind="Internal").ap()
    d_xmt = nc.dram_tensor("scr_xmt", [4096, D], BF16, kind="Internal").ap()
    t_scr = nc.dram_tensor("scr_attn", [D, N], BF16, kind="Internal")
    d_scr = t_scr.ap()
    d_scr_rows = t_scr.ap().rearrange("d (g n) -> (d g) n", g=2)

    dbg = {}

    def mkdbg(name, shape, dt=F32):
        if debug:
            dbg[name] = nc.dram_tensor("dbg_" + name, shape, dt,
                                       kind="ExternalOutput").ap()

    mkdbg("rstd0", [128, TT]); mkdbg("ss0", [128, KC, TT])
    mkdbg("sr0", [128, KC, TT])
    mkdbg("k0", [128, H, TT]); mkdbg("v0", [128, H, TT])
    mkdbg("q0", [128, H, TT])
    mkdbg("sig0", [8, 2, TT]); mkdbg("sg0", [8, 16])
    mkdbg("a0", [128, TT]); mkdbg("sd0", [128, TT]); mkdbg("g00", [128, TT])
    mkdbg("dz0", [128, TT]); mkdbg("gm64", [128, 64])
    mkdbg("gtm0", [128, 4, 128]); mkdbg("ktm0", [128, 4, 128])
    mkdbg("wc00", [128, 128]); mkdbg("wc10", [128, 128])
    mkdbg("vst0", [128, H, TT]); mkdbg("comb0", [128, KC, TT])
    mkdbg("xattn0", [128, KC, ST]); mkdbg("xmid0", [128, KC, ST])
    mkdbg("xffn0", [128, KC, ST])
    mkdbg("cnts", [8, 32]); mkdbg("offs", [8, 32]); mkdbg("grpexp", [1, 24])
    mkdbg("sid", [6144, 1], I32); mkdbg("gat", [4096, 1])
    mkdbg("xtm", [4096, D], BF16); mkdbg("ytm", [4096, D], BF16)

    es = ExitStack()

    def sb(name, shape, dt=F32):
        return es.enter_context(nc.sbuf_tensor(name, shape, dt))

    with es:
        PB = [es.enter_context(nc.psum_tensor(f"pb{i}", [128, 512], F32))
              for i in range(8)]

        def PBbf(i):
            return PB[i][:].bitcast(BF16)

        ones_f = sb("ones_f", [128, 128])
        ones_bf = sb("ones_bf", [128, 128], BF16)
        id_f = sb("id_f", [128, 128])
        id_bf = sb("id_bf", [128, 128], BF16)
        oh8_f = sb("oh8_f", [8, 8 * 128], BF16)
        ohc8_bf = sb("ohc8_bf", [128, 64], BF16)
        bd8_f = sb("bd8_f", [8, 64])
        rw_bf = sb("rw_bf", [128, KC, NE], BF16)
        rb_bf = sb("rb_bf", [1, NE], BF16)
        gidx_sb = sb("gidx_sb", [128, KC], I32)

        c_in = Chan(es, nc, "c_in")
        c_ing = Chan(es, nc, "c_ing")
        c_xg = Chan(es, nc, "c_xg")
        c_dbg = Chan(es, nc, "c_dbg")
        c_pe = Chan(es, nc, "c_pe")
        c_v = Chan(es, nc, "c_v")
        c_a = Chan(es, nc, "c_a")
        c_x = Chan(es, nc, "c_x")
        c_w = Chan(es, nc, "c_w")
        c_w1 = Chan(es, nc, "c_w1")
        c_w2 = Chan(es, nc, "c_w2")
        c_g = Chan(es, nc, "c_g")
        c_st = Chan(es, nc, "c_st")
        c_out = Chan(es, nc, "c_out")
        c_wc = Chan(es, nc, "c_wc")
        c_st2 = Chan(es, nc, "c_st2")
        c_w2a = Chan(es, nc, "c_w2a")
        c_w2b = Chan(es, nc, "c_w2b")
        c_w2c = Chan(es, nc, "c_w2c")
        c_scl = [Chan(es, nc, f"c_scl{i}") for i in range(3)]
        c_sx = [Chan(es, nc, f"c_sx{i}") for i in range(3)]
        c_gf = Chan(es, nc, "c_gf")

        conv_state = {"i": 0}

        def conv_drip(gp2, n=1):
            for _ in range(n):
                k = conv_state["i"]
                if k >= 128:
                    return
                conv_state["i"] = k + 1
                ti, j = divmod(k, 8)
                if ti < NE:
                    npr = D // 8
                    c_wc.inc(gp2.dma_start(
                        d_w1b[ti * D + j * npr:ti * D + (j + 1) * npr, :],
                        d_w1[ti][j * npr:(j + 1) * npr, :]), 16)
                else:
                    e = ti - NE
                    npr = DFF // 8
                    c_wc.inc(gp2.dma_start(
                        d_w2b[e * DFF + j * npr:e * DFF + (j + 1) * npr, :],
                        d_w2[e][j * npr:(j + 1) * npr, :]), 16)

        def conv_chunk(gp2, ti, pieces=8):
            conv_drip(gp2, pieces)

        def dripg(blk):
            blk.gpsimd(lambda gp2: conv_drip(gp2, 1))

        with nc.Block(no_gpsimd_drain=True) as blk:
            def _p0sync(sy):
                for dst, srcd in [
                    (ones_f[:], d_ones), (id_f[:], d_id), (bd8_f[:], d_bd8),
                    (gidx_sb[:], d_gidx),
                ]:
                    c_in.inc(sy.dma_start(dst, srcd), 16)
                sy.wait_ge(c_in.sem, c_in.n)
            blk.sync(_p0sync)

            def _p0gp(gp):
                for dst, srcd in [
                    (rw_bf[:], d_rw.rearrange("(k p) c -> p k c", p=128)),
                    (oh8_f[:], d_oh8),
                    (rb_bf[:], d_rb), (ohc8_bf[:], d_ohc8),
                ]:
                    c_ing.inc(gp.dma_start(dst, srcd), 16)
                gp.wait_ge(c_ing.sem, c_ing.n)
            blk.gpsimd(_p0gp)

        with nc.Block(no_gpsimd_drain=True) as blk:
            def _p0b(v):
                v.tensor_copy(ones_bf[:], ones_f[:])
                v.tensor_copy(id_bf[:], id_f[:])
            blk.vector(_p0b)

            def _p0sc(sc):
                sc.preload_act_table(AF.Sigmoid)
            blk.scalar(_p0sc)

        if attn:
            p1 = ExitStack()
            with p1:
                def s1(name, shape, dt=F32):
                    return p1.enter_context(nc.sbuf_tensor(name, shape, dt))

                wkv_bf = s1("wkv_bf", [128, KC, 2 * D], BF16)
                wq_bf = s1("wq_bf", [128, KC, D], BF16)
                wcomb_bf = s1("wcomb_bf", [128, KC, D], BF16)
                wsg_bf = s1("wsg_bf", [128, KC, 16], BF16)
                wmd_bf = s1("wmd_bf", [128, KC, 16], BF16)
                smg_f = s1("smg_f", [128, KC])
                rmg_f = s1("rmg_f", [128, KC])
                mw0_f = s1("mw0_f", [128, 128])
                mw1_f = s1("mw1_f", [128, 128])
                mw0_bf = s1("mw0_bf", [128, 128], BF16)
                mw1_bf = s1("mw1_bf", [128, 128], BF16)
                w1t_bf = s1("w1t_bf", [128, 128], BF16)
                mhg1_f = s1("mhg1_f", [128, H])
                xt = [s1(f"xt{i}", [128, KC, TT], BF16) for i in range(2)]
                xga = {}
                rstd = [s1(f"rstd{i}", [128, TT]) for i in range(2)]
                ssb = s1("ssb", [128, KC, TT], BF16)
                srb = s1("srb", [128, KC, TT], BF16)
                vstage = ssb
                vsq = srb
                sqb_lo = ssb[:].rearrange("p h t -> p (h t)").bitcast(F32)
                sqb_hi = srb[:].rearrange("p h t -> p (h t)").bitcast(F32)

                def sqb(kc):
                    v_ = sqb_lo if kc < 4 else sqb_hi
                    return v_[:, (kc % 4) * TT:(kc % 4 + 1) * TT]
                kb = s1("kb", [128, H, TT], BF16)
                vf = kb
                vb = s1("vb", [128, H, TT], BF16)
                cstage = vb
                _vbf = vb[:].rearrange("p h t -> p (h t)").bitcast(F32)
                nt1 = _vbf[:, 0:TT]
                nt2 = _vbf[:, TT:2 * TT]
                qb = s1("qb", [128, H, TT], BF16)
                sig = s1("sig", [8, 2, TT], BF16)
                lrtm = s1("lrtm", [128, 4, 8])
                cmb = s1("cmb", [128, KC, 8], BF16)
                cmf = s1("cmf", [128, KC, 8])
                sgt = s1("sgt", [8, 16])
                rhs8 = s1("rhs8", [8, 2, 64])
                gm64 = s1("gm64", [128, 64]); gd64 = s1("gd64", [128, 64])
                sgz = s1("sgz", [128, H, TT], BF16)
                gtm = sgz[:].rearrange("p h (s q) -> p h s q", s=4)
                sdb = s1("sdb", [128, H, TT], BF16)
                dztm = sdb[:].rearrange("p h (s q) -> p h s q", s=4)
                ab = s1("ab", [128, H, TT], BF16)
                wcb = ab[:].rearrange("p h t -> p (h t)").rearrange(
                    "p (a b c d) -> p a b c d", a=2, b=2, c=H)
                g0b = s1("g0b", [128, H, TT], BF16)
                _g0f = g0b[:].rearrange("p h t -> p (h t)")
                sgx = _g0f[:, 0:512].rearrange("p (h c) -> p h c", h=H)
                sxb = _g0f[:, 512:1024].rearrange("p (h c) -> p h c", h=H)
                comb8 = _g0f[0:8, 1024:1536]
                dzb = s1("dzb", [128, H, TT], BF16)
                _dzf = dzb[:].rearrange("p h t -> p (h t)").bitcast(F32)
                n8a = _dzf[0:8, 0:TT]
                n8b = _dzf[0:8, TT:2 * TT]
                rstd8 = _dzf[0:8, 2 * TT:3 * TT]
                n8c = _dzf[0:8, 3 * TT:4 * TT]
                ktm = s1("ktm", [128, H, 4, 128], BF16)
                atm = s1("atm", [128, H, 4, 128], BF16)
                xsgA = ktm[:].rearrange("p h s q -> p (h s q)").bitcast(
                    F32).rearrange("p (k t) -> p k t", t=TT)
                xsgB = atm[:].rearrange("p h s q -> p (h s q)").bitcast(
                    F32).rearrange("p (k t) -> p k t", t=TT)
                m0 = s1("m0", [128, H, 128]); m1 = s1("m1", [128, H, 128])
                u0 = s1("u0", [128, H, 128]); u1 = s1("u1", [128, H, 128])
                mw0e = mw0_f[:].unsqueeze(1).to_broadcast([128, H, 128])
                mw1e = mw1_f[:].unsqueeze(1).to_broadcast([128, H, 128])
                zzb = cstage[:, :, TT - 64:TT]

                with nc.Block(no_gpsimd_drain=True) as blk:
                    def _w1s(sy):
                        for dst, srcd in [
                            (smg_f[:], d_smg), (rmg_f[:], d_rmg),
                            (mw0_f[:], d_mw0), (mw1_f[:], d_mw1),
                            (mhg1_f[:], d_mhg),
                        ]:
                            c_in.inc(sy.dma_start(dst, srcd), 16)
                        sy.wait_ge(c_in.sem, c_in.n)
                    blk.sync(_w1s)

                    def _w1g(gp):
                        for dst, srcd in [
                            (wkv_bf[:],
                             d_wkv.rearrange("(k p) c -> p k c", p=128)),
                            (wq_bf[:],
                             d_wq.rearrange("(k p) c -> p k c", p=128)),
                            (wcomb_bf[:],
                             d_wcomb.rearrange("(k p) c -> p k c", p=128)),
                            (wsg_bf[:],
                             d_wsg.rearrange("(k p) c -> p k c", p=128)),
                            (wmd_bf[:],
                             d_wmd.rearrange("(k p) c -> p k c", p=128)),
                            (mw0_bf[:], d_mw0), (mw1_bf[:], d_mw1),
                        ]:
                            c_ing.inc(gp.dma_start(dst, srcd), 16)
                        gp.wait_ge(c_ing.sem, c_ing.n)
                    blk.gpsimd(_w1g)
                with nc.Block(no_gpsimd_drain=True) as blk:
                    def _w1t(t):
                        t.transpose(PBbf(0)[:, 0:128], mw1_bf[:], id_bf[:])
                    blk.tensor(_w1t)

                    def _w1v(v):
                        v.tensor_scalar(mhg1_f[:], mhg1_f[:], 1.0, None,
                                        ALU.add)
                        v.tensor_scalar(wmd_bf[:], wmd_bf[:], 1.0 / 64.0,
                                        None, ALU.mult)
                        v.memset(m0[:], 0.0); v.memset(m1[:], 0.0)
                        v.memset(u0[:], 0.0); v.memset(u1[:], 0.0)
                        v.memset(cstage[:, :, :], 0.0)
                    blk.vector(_w1v)
                with nc.Block(no_gpsimd_drain=True) as blk:
                    def _w1c(v):
                        v.tensor_copy(w1t_bf[:], PBbf(0)[:, 0:128])
                    blk.vector(_w1c)

                    def _pl(gp2):
                        conv_drip(gp2, 1)
                    blk.gpsimd(_pl)

                    def _pls(sy):
                        c_x.inc(sy.dma_start(
                            xsgA[:],
                            d_seq.rearrange("(k p) n -> p k n", p=128)
                            [:, 0:4, 0:TT]), 16)
                        c_x.inc(sy.dma_start(
                            xsgB[:],
                            d_seq.rearrange("(k p) n -> p k n", p=128)
                            [:, 4:8, 0:TT]), 16)
                        xga[0] = c_x.n
                    blk.sync(_pls)

                    def _plz(sy):
                        c_st.inc(sy.dma_start(
                            d_scr.rearrange("(k p) n -> p k n", p=128)
                            [:, :, 0:63],
                            cstage[:, :, 0:63]), 16)
                        sy.wait_ge(c_st.sem, c_st.n)
                    blk.sync(_plz)

                def stats_blocks(xtile, rout, xmark):
                    with nc.Block(no_gpsimd_drain=True) as blk:
                        def _cst(sc, xtile=xtile, xmark=xmark):
                            sc.wait_ge(c_x.sem, xmark)
                            for kc in range(KC):
                                srcv = xsgA if kc < 4 else xsgB
                                c_a.inc(sc.copy(xtile[:, kc, :],
                                                srcv[:, kc % 4, :]))
                            for kc in range(KC):
                                srcv = xsgA if kc < 4 else xsgB
                                c_a.inc(sc.activation(
                                    sqb(kc), srcv[:, kc % 4, :], AF.Square))
                        blk.scalar(_cst)

                        def _mm(t):
                            t.wait_ge(c_a.sem, c_a.n)
                            for kc in range(KC):
                                t.matmul(PB[7][:], ones_f[:], sqb(kc),
                                         start=(kc == 0), stop=(kc == KC - 1))
                        blk.tensor(_mm)
                    with nc.Block(no_gpsimd_drain=True) as blk:
                        def _nw(v, rout=rout):
                            def e(inst):
                                c_v.inc(inst); c_v.wait(v)
                            e(v.tensor_scalar(nt1, PB[7][:], 1.0 / D, 1e-6,
                                              ALU.mult, ALU.add))
                            _rsqrt(v, c_v, rout[:], nt1, nt2, sqb(0))
                        blk.vector(_nw)

                stats_blocks(xt[0], rstd[0], xga[0])

                for ti in range(min(NT, ntile)):
                    cur, nxt = ti % 2, (ti + 1) % 2

                    if ti + 1 < NT:
                        with nc.Block(no_gpsimd_drain=True) as blk:
                            def _pf(gp2):
                                conv_drip(gp2, 1)
                            blk.gpsimd(_pf)

                            def _pfs(sy, ti=ti):
                                c_x.inc(sy.dma_start(
                                    xsgA[:],
                                    d_seq.rearrange("(k p) n -> p k n", p=128)
                                    [:, 0:4,
                                     (ti + 1) * TT:(ti + 2) * TT]), 16)
                                c_x.inc(sy.dma_start(
                                    xsgB[:],
                                    d_seq.rearrange("(k p) n -> p k n", p=128)
                                    [:, 4:8,
                                     (ti + 1) * TT:(ti + 2) * TT]), 16)
                                xga[ti + 1] = c_x.n
                            blk.sync(_pfs)
                        stats_blocks(xt[nxt], rstd[nxt], xga[ti + 1])

                    if ti == NT - 1:
                        with nc.Block(no_gpsimd_drain=True) as blk:
                            def _zz(v):
                                v.memset(zzb, 0.0)
                            blk.vector(_zz)
                    if p1cut < 2:
                        continue
                    with nc.Block(no_gpsimd_drain=True) as blk:
                        def _ssr(v, cur=cur, nxt=nxt, ti=ti):
                            for kc in range(KC):
                                v.scalar_tensor_tensor(
                                    ssb[:, kc, :], xt[cur][:, kc, :],
                                    smg_f[:, kc:kc + 1], rstd[cur][:],
                                    ALU.mult, ALU.mult)
                            for kc in range(KC):
                                v.scalar_tensor_tensor(
                                    srb[:, kc, 0:TT - 63],
                                    xt[cur][:, kc, 63:TT],
                                    rmg_f[:, kc:kc + 1], rstd[cur][:, 63:TT],
                                    ALU.mult, ALU.mult)
                            if ti + 1 < NT:
                                for kc in range(KC):
                                    v.scalar_tensor_tensor(
                                        srb[:, kc, TT - 63:TT],
                                        xt[nxt][:, kc, 0:63],
                                        rmg_f[:, kc:kc + 1],
                                        rstd[nxt][:, 0:63],
                                        ALU.mult, ALU.mult)
                            else:
                                for kc in range(KC):
                                    v.tensor_copy(srb[:, kc, TT - 63:TT],
                                                  zzb[:, kc, 0:63])
                        blk.vector(_ssr)
                        dripg(blk)

                    if p1cut < 3:
                        continue
                    with nc.Block(no_gpsimd_drain=True) as blk:
                        a_base = c_a.n

                        def _ckv(t, a_base=a_base):
                            for oc in range(16):
                                if oc >= 4:
                                    t.wait_ge(c_a.sem, a_base + oc - 3)
                                for kc in range(KC):
                                    ii = t.matmul(PB[oc % 4][:],
                                                  wkv_bf[:, kc,
                                                         oc * 128:
                                                         (oc + 1) * 128],
                                                  ssb[:, kc, :],
                                                  start=(kc == 0),
                                                  stop=(kc == KC - 1))
                                c_pe.inc(ii)
                            for oc in range(8):
                                if oc >= 2:
                                    t.wait_ge(c_a.sem, a_base + 15 + oc)
                                for kc in range(KC):
                                    ii = t.matmul(PB[4 + oc % 2][:],
                                                  wq_bf[:, kc,
                                                        oc * 128:
                                                        (oc + 1) * 128],
                                                  srb[:, kc, :],
                                                  start=(kc == 0),
                                                  stop=(kc == KC - 1))
                                c_pe.inc(ii)
                            for kc in range(KC):
                                ii = t.matmul(PB[6][0:8, :],
                                              wsg_bf[:, kc, 0:8],
                                              ssb[:, kc, :],
                                              start=(kc == 0),
                                              stop=(kc == KC - 1))
                            c_pe.inc(ii)
                            for kc in range(KC):
                                ii = t.matmul(PB[7][0:8, :],
                                              wsg_bf[:, kc, 8:16],
                                              srb[:, kc, :],
                                              start=(kc == 0),
                                              stop=(kc == KC - 1))
                            c_pe.inc(ii)
                        blk.tensor(_ckv)

                        pe_base = c_pe.n - 26

                        def _cev(sc, pe_base=pe_base):
                            for oc in range(16):
                                sc.wait_ge(c_pe.sem, pe_base + oc + 1)
                                dstt = kb if oc < 8 else vb
                                c_a.inc(sc.copy(dstt[:, oc % 8, :],
                                                PB[oc % 4][:]))
                            for oc in range(8):
                                sc.wait_ge(c_pe.sem, pe_base + 16 + oc + 1)
                                c_a.inc(sc.copy(qb[:, oc, :],
                                                PB[4 + oc % 2][:]))
                            sc.wait_ge(c_pe.sem, pe_base + 25)
                            c_a.inc(sc.activation(sig[:, 0, :], PB[6][0:8, :],
                                                  AF.Sigmoid))
                            sc.wait_ge(c_pe.sem, pe_base + 26)
                            c_a.inc(sc.activation(sig[:, 1, :], PB[7][0:8, :],
                                                  AF.Sigmoid))
                        blk.scalar(_cev)

                        def _ccm(v):
                            for kc in range(KC):
                                ii = v.tensor_reduce(
                                    cmf[:, kc, :],
                                    ssb[:, kc, :]
                                    .rearrange("p (c u) -> p c u", c=8),
                                    AX.X, ALU.add)
                            c_v.inc(ii); c_v.wait(v)
                            for kc in range(KC):
                                v.tensor_copy(cmb[:, kc, :], cmf[:, kc, :])
                        blk.vector(_ccm)
                        dripg(blk)

                    if p1cut < 4:
                        continue
                    with nc.Block(no_gpsimd_drain=True) as blk:
                        def _g1(t):
                            for kc in range(KC):
                                t.matmul(PB[0][0:8, 0:8], wmd_bf[:, kc, 0:8],
                                         cmb[:, kc, :],
                                         start=(kc == 0), stop=(kc == KC - 1))
                            for kc in range(KC):
                                ii = t.matmul(PB[0][0:8, 8:16],
                                              wmd_bf[:, kc, 8:16],
                                              cmb[:, kc, :],
                                              start=(kc == 0),
                                              stop=(kc == KC - 1))
                            c_pe.inc(ii)
                            for sub in range(4):
                                ii = t.transpose(
                                    PBbf(1)[:, sub * 8:(sub + 1) * 8],
                                    sig[0:8, 0, sub * 128:(sub + 1) * 128],
                                    id_bf[0:8, 0:8])
                            c_pe.inc(ii)
                        blk.tensor(_g1)

                        pe_g = c_pe.n

                        def _g2(sc, pe_g=pe_g):
                            sc.wait_ge(c_pe.sem, pe_g - 1)
                            c_a.inc(sc.activation(sgt[:, 0:8],
                                                  PB[0][0:8, 0:8], AF.Sigmoid))
                            c_a.inc(sc.activation(sgt[:, 8:16],
                                                  PB[0][0:8, 8:16],
                                                  AF.Sigmoid, scale=-1.0))
                        blk.scalar(_g2)

                        a_g = c_a.n

                        def _g3(v, pe_g=pe_g, a_g=a_g):
                            v.wait_ge(c_pe.sem, pe_g)
                            v.tensor_copy(
                                lrtm[:],
                                PBbf(1)[:, 0:32].rearrange("p (s c) -> p s c",
                                                           s=4))
                            v.wait_ge(c_a.sem, a_g)
                            bd3 = bd8_f[:].rearrange("h (c u) -> h c u",
                                                     u=8)
                            v.tensor_mul(rhs8[:, 0, :]
                                         .rearrange("h (c u) -> h c u", u=8),
                                         sgt[:, 0:8].unsqueeze(2)
                                         .to_broadcast([8, 8, 8]), bd3)
                            v.tensor_mul(rhs8[:, 1, :]
                                         .rearrange("h (c u) -> h c u", u=8),
                                         sgt[:, 8:16].unsqueeze(2)
                                         .to_broadcast([8, 8, 8]), bd3)
                        blk.vector(_g3)
                        dripg(blk)
                    with nc.Block(no_gpsimd_drain=True) as blk:
                        def _g4(t):
                            t.matmul(PB[2][:, 0:64], ones_f[0:8, :],
                                     rhs8[:, 0, :], start=True, stop=True)
                            c_pe.inc(t.matmul(PB[3][:, 0:64], ones_f[0:8, :],
                                              rhs8[:, 1, :], start=True,
                                              stop=True))
                        blk.tensor(_g4)

                        pe_g4 = c_pe.n

                        def _g5(v, pe_g4=pe_g4):
                            v.wait_ge(c_pe.sem, pe_g4)
                            v.tensor_copy(gm64[:], PB[2][:, 0:64])
                            v.tensor_copy(gd64[:], PB[3][:, 0:64])
                        blk.vector(_g5)
                        dripg(blk)

                    if p1cut < 5:
                        continue
                    for hg in range(2):
                        hs = list(range(hg * 4, hg * 4 + 4))
                        with nc.Block(no_gpsimd_drain=True) as blk:
                            def _d1(t, hs=hs):
                                for h in hs:
                                    c_pe.inc(t.matmul(PB[h % 4][:], mw0_bf[:],
                                                      kb[:, h, :], start=True,
                                                      stop=True))
                            blk.tensor(_d1)

                            pe_d = c_pe.n - 4

                            def _d2(sc, hs=hs, pe_d=pe_d):
                                for i, h in enumerate(hs):
                                    sc.wait_ge(c_pe.sem, pe_d + i + 1)
                                    c_a.inc(sc.activation(sgz[:, h, :],
                                                          PB[h % 4][:],
                                                          AF.Sigmoid))
                            blk.scalar(_d2)

                            a_d = c_a.n - 4

                            def _d3(v, hs=hs, a_d=a_d):
                                for i, h in enumerate(hs):
                                    v.wait_ge(c_a.sem, a_d + i + 1)
                                    ii = v.tensor_mul(ab[:, h, :],
                                                      PB[h % 4][:],
                                                      sgz[:, h, :])
                                c_v.inc(ii); c_v.wait(v)
                                for h in hs:
                                    ii = v.tensor_mul(sdb[:, h, :],
                                                      ab[:, h, :],
                                                      sgz[:, h, :])
                                c_v.inc(ii); c_v.wait(v)
                                for h in hs:
                                    ii = v.tensor_sub(sdb[:, h, :],
                                                      ab[:, h, :],
                                                      sdb[:, h, :])
                                c_v.inc(ii); c_v.wait(v)
                                for h in hs:
                                    v.tensor_add(sdb[:, h, :], sgz[:, h, :],
                                                 sdb[:, h, :])
                            blk.vector(_d3)
                            dripg(blk)
                        with nc.Block(no_gpsimd_drain=True) as blk:
                            def _d4(t, hs=hs):
                                for h in hs:
                                    c_pe.inc(t.matmul(PB[4 + h % 4][:],
                                                      mw1_bf[:],
                                                      ab[:, h, :], start=True,
                                                      stop=True))
                            blk.tensor(_d4)

                            pe_d4 = c_pe.n - 4

                            def _d5(v, hs=hs, pe_d4=pe_d4):
                                for i, h in enumerate(hs):
                                    v.wait_ge(c_pe.sem, pe_d4 + i + 1)
                                    v.tensor_sub(g0b[:, h, :],
                                                 PB[4 + h % 4][:], vb[:, h, :])
                            blk.vector(_d5)
                            dripg(blk)
                        with nc.Block(no_gpsimd_drain=True) as blk:
                            def _d6(t, hs=hs):
                                for h in hs:
                                    c_pe.inc(t.matmul(PB[h % 4][:], w1t_bf[:],
                                                      g0b[:, h, :],
                                                      start=True, stop=True))
                            blk.tensor(_d6)

                            pe_d6 = c_pe.n - 4

                            def _d7(v, hs=hs, pe_d6=pe_d6):
                                for i, h in enumerate(hs):
                                    v.wait_ge(c_pe.sem, pe_d6 + i + 1)
                                    v.tensor_mul(dzb[:, h, :], PB[h % 4][:],
                                                 sdb[:, h, :])
                            blk.vector(_d7)
                            dripg(blk)

                    if p1cut < 6:
                        continue
                    if debug and ti == 0:
                        with nc.Block(no_gpsimd_drain=True) as blk:
                            def _dbgE(gp9):
                                for nm, srcd in [("a0", ab[:, 0, :]),
                                                 ("sd0", sdb[:, 0, :]),
                                                 ("g00", g0b[:, 0, :]),
                                                 ("dz0", dzb[:, 0, :])]:
                                    if nm in dbg:
                                        c_dbg.inc(gp9.dma_start(dbg[nm],
                                                                srcd), 16)
                                gp9.wait_ge(c_dbg.sem, c_dbg.n)
                            blk.gpsimd(_dbgE)

                    if p1cut < 6:
                        continue
                    with nc.Block(no_gpsimd_drain=True) as blk:
                        srcs = []
                        for src, dst, scaled in ((kb, ktm, False),
                                                 (ab, atm, False),
                                                 (g0b, gtm, True),
                                                 (dzb, dztm, True)):
                            for h in range(H):
                                srcs.append((src, dst, scaled, h))
                        a_base = c_a.n
                        v_base = c_v.n

                        def _e1(t, srcs=srcs, a_base=a_base, v_base=v_base):
                            for gi, (src, dst, scaled, h) in enumerate(srcs):
                                bank = gi % 4
                                if 4 <= gi < 20:
                                    t.wait_ge(c_a.sem, a_base + gi - 3)
                                elif gi >= 20:
                                    t.wait_ge(c_v.sem, v_base + gi - 19)
                                for sq_ in range(4):
                                    ii = t.transpose(
                                        PBbf(bank)[:, sq_ * 256:
                                                   sq_ * 256 + 128],
                                        src[:, h,
                                            sq_ * 128:(sq_ + 1) * 128],
                                        id_bf[:])
                                c_pe.inc(ii)
                        blk.tensor(_e1)

                        pe_e = c_pe.n - len(srcs)

                        def _e2s(sc, srcs=srcs, pe_e=pe_e):
                            for gi, (src, dst, scaled, h) in enumerate(srcs):
                                if scaled:
                                    continue
                                bank = gi % 4
                                sc.wait_ge(c_pe.sem, pe_e + gi + 1)
                                pv = PBbf(bank).rearrange(
                                    "p (s q) -> p s q", s=4)[:, :, 0:128]
                                c_a.inc(sc.copy(dst[:, h, :, :], pv))
                        blk.scalar(_e2s)

                        def _e2(v, srcs=srcs, pe_e=pe_e):
                            for gi, (src, dst, scaled, h) in enumerate(srcs):
                                if not scaled:
                                    continue
                                bank = gi % 4
                                v.wait_ge(c_pe.sem, pe_e + gi + 1)
                                pv = PBbf(bank).rearrange(
                                    "p (s q) -> p s q", s=4)[:, :, 0:128]
                                ii = v.scalar_tensor_tensor(
                                    dst[:, h, :, :], pv, CS,
                                    lrtm[:, :, h:h + 1]
                                    .to_broadcast([128, 4, 128]),
                                    ALU.mult, ALU.mult)
                                c_v.inc(ii)
                        blk.vector(_e2)
                        dripg(blk)

                    if p1cut < 7:
                        continue
                    with nc.Block(no_gpsimd_drain=True) as blk:
                        pe_dwa = [0] * 8; pe_dwb = [0] * 8
                        pe_x0 = [0] * 8; pe_vl = [0] * 8
                        v_sx = [0] * 8; v_msub = [0] * 8; v_w = [0] * 8
                        a_sgx = [0] * 8; a_vev = [0] * 8
                        cc = c_pe.n
                        for cl in range(8):
                            cc += 1; pe_dwa[cl] = cc
                            cc += 1; pe_dwb[cl] = cc
                            if cl >= 1:
                                cc += 1; pe_x0[cl - 1] = cc
                                cc += 1; pe_vl[cl - 1] = cc
                        cc += 1; pe_x0[7] = cc
                        cc += 1; pe_vl[7] = cc
                        vv = c_v.n
                        for cl in range(8):
                            vv += 1
                            if cl >= 1:
                                vv += 1; v_sx[cl - 1] = vv
                            vv += 1; v_msub[cl] = vv
                            vv += 1
                            vv += 1
                            vv += 1; v_w[cl] = vv
                        vv += 1; v_sx[7] = vv
                        aa = c_a.n
                        for cl in range(8):
                            aa += 1; a_sgx[cl] = aa
                            if cl >= 1:
                                aa += 1; a_vev[cl - 1] = aa
                        aa += 1; a_vev[7] = aa
                        sgf = _g0f[:, 0:512]
                        sxf = _g0f[:, 512:1024]
                        _xtf = xt[cur][:].rearrange(
                            "p k t -> p (k t)").bitcast(F32)
                        tmp0 = _xtf[:, 0:1024].rearrange(
                            "p (h c) -> p h c", h=H)
                        tmp1 = _xtf[:, 1024:2048].rearrange(
                            "p (h c) -> p h c", h=H)

                        def pb4(i):
                            return PB[i][:].rearrange("p (h c) -> p h c", h=4)

                        def _f_pe(t):
                            def dw(cl):
                                sq_ = cl // 2
                                rb_ = (cl % 2) * 64
                                if cl >= 1:
                                    t.wait_ge(c_v.sem, v_msub[cl - 1])
                                for hh in range(2):
                                    for h in range(hh * 4, hh * 4 + 4):
                                        co = (h % 4) * 128
                                        t.matmul(
                                            PB[hh][:, co:co + 128],
                                            ktm[rb_:rb_ + 64, h, sq_, :],
                                            dztm[rb_:rb_ + 64, h, sq_, :],
                                            start=True, stop=True)
                                        ii = t.matmul(
                                            PB[2 + hh][:, co:co + 128],
                                            atm[rb_:rb_ + 64, h, sq_, :],
                                            gtm[rb_:rb_ + 64, h, sq_, :],
                                            start=True, stop=True)
                                    c_pe.inc(ii)
                                assert c_pe.n == pe_dwb[cl]

                            def retr(cl):
                                par = cl % 2
                                xb_ = 4 if par == 0 else 6
                                vb2 = 5 if par == 0 else 7
                                t.wait_ge(c_v.sem, v_w[cl])
                                if cl >= 2:
                                    t.wait_ge(c_a.sem, a_sgx[cl - 2])
                                for h in range(H):
                                    ii = t.matmul(
                                        PB[xb_][:, h * 64:h * 64 + 64],
                                        wcb[:, par, 0, h, :],
                                        qb[:, h, cl * 64:cl * 64 + 64],
                                        start=True, stop=True)
                                c_pe.inc(ii)
                                assert c_pe.n == pe_x0[cl]
                                t.wait_ge(c_v.sem, v_sx[cl])
                                if cl >= 2:
                                    t.wait_ge(c_a.sem, a_vev[cl - 2])
                                for h in range(H):
                                    ii = t.matmul(
                                        PB[vb2][:, h * 64:h * 64 + 64],
                                        wcb[:, par, 1, h, :],
                                        sxb[:, h, :],
                                        start=True, stop=True)
                                c_pe.inc(ii)
                                assert c_pe.n == pe_vl[cl]

                            for cl in range(8):
                                dw(cl)
                                if cl >= 1:
                                    retr(cl - 1)
                            retr(7)
                        blk.tensor(_f_pe)

                        gg = c_gf.n
                        g_s = [0] * 8; g_u = [0] * 8
                        for cl in range(8):
                            gg += 1; g_s[cl] = gg
                            gg += 1
                            gg += 1; g_u[cl] = gg

                        def _f_v(v):
                            for cl in range(8):
                                par = cl % 2
                                c8 = cl * 8
                                gmbc = gm64[:, c8:c8 + 8].unsqueeze(2) \
                                    .to_broadcast([128, H, 128])
                                gdbc = gd64[:, c8:c8 + 8].unsqueeze(2) \
                                    .to_broadcast([128, H, 128])
                                c_v.inc(v.tensor_mul(m0[:], m0[:], gmbc))
                                if cl >= 1:
                                    pxb = 4 if (cl - 1) % 2 == 0 else 6
                                    v.wait_ge(c_a.sem, a_sgx[cl - 1])
                                    if cl >= 2:
                                        v.wait_ge(c_pe.sem, pe_vl[cl - 2])
                                    c_v.inc(v.tensor_mul(
                                        sxf, PB[pxb][:, 0:512], sgf))
                                    assert c_v.n == v_sx[cl - 1]
                                v.wait_ge(c_pe.sem, pe_dwa[cl])
                                v.wait_ge(c_gf.sem, g_s[cl])
                                c_v.wait(v)
                                v.tensor_sub(m0[:, 0:4, :], m0[:, 0:4, :],
                                             pb4(0))
                                v.tensor_sub(m1[:, 0:4, :], m1[:, 0:4, :],
                                             pb4(2))
                                v.wait_ge(c_pe.sem, pe_dwb[cl])
                                v.tensor_sub(m0[:, 4:8, :], m0[:, 4:8, :],
                                             pb4(1))
                                ii = v.tensor_sub(m1[:, 4:8, :],
                                                  m1[:, 4:8, :], pb4(3))
                                c_v.inc(ii)
                                assert c_v.n == v_msub[cl]
                                c_v.wait(v)
                                c_v.inc(v.tensor_mul(tmp0, u0[:], gdbc))
                                c_v.wait(v)
                                c_v.inc(v.tensor_add(u0[:], tmp0, m0[:]))
                                c_v.wait(v)
                                if cl >= 2:
                                    v.wait_ge(c_pe.sem, pe_vl[cl - 2])
                                v.tensor_add(wcb[:, par, 0, :, :], u0[:],
                                             mw0e)
                                v.wait_ge(c_gf.sem, g_u[cl])
                                ii = v.tensor_add(wcb[:, par, 1, :, :],
                                                  u1[:], mw1e)
                                c_v.inc(ii)
                                assert c_v.n == v_w[cl]
                            v.wait_ge(c_a.sem, a_sgx[7])
                            v.wait_ge(c_pe.sem, pe_vl[6])
                            c_v.inc(v.tensor_mul(sxf, PB[6][:, 0:512], sgf))
                            assert c_v.n == v_sx[7]
                        blk.vector(_f_v)

                        f_v0 = c_v.n - 48

                        def _f_g(gp2, f_v0=f_v0):
                            conv_drip(gp2, 1)
                            gp2.wait_ge(c_v.sem, f_v0)
                            for cl in range(8):
                                c8 = cl * 8
                                gmbc = gm64[:, c8:c8 + 8].unsqueeze(2) \
                                    .to_broadcast([128, H, 128])
                                gdbc = gd64[:, c8:c8 + 8].unsqueeze(2) \
                                    .to_broadcast([128, H, 128])
                                if cl >= 1:
                                    gp2.wait_ge(c_v.sem, v_msub[cl - 1])
                                c_gf.inc(gp2.tensor_tensor(
                                    m1[:], m1[:], gmbc, ALU.mult))
                                assert c_gf.n == g_s[cl]
                                c_gf.inc(gp2.tensor_tensor(
                                    tmp1, u1[:], gdbc, ALU.mult))
                                c_gf.wait(gp2)
                                gp2.wait_ge(c_v.sem, v_msub[cl])
                                if cl >= 1:
                                    gp2.wait_ge(c_v.sem, v_w[cl - 1])
                                c_gf.inc(gp2.tensor_tensor(
                                    u1[:], tmp1, m1[:], ALU.add))
                                assert c_gf.n == g_u[cl]
                        blk.gpsimd(_f_g)

                        def _f_a(sc):
                            for cl in range(8):
                                par = cl % 2
                                xb_ = 4 if par == 0 else 6
                                if cl >= 1:
                                    sc.wait_ge(c_v.sem, v_sx[cl - 1])
                                sc.wait_ge(c_pe.sem, pe_x0[cl])
                                c_a.inc(sc.activation(
                                    sgf, PB[xb_][:, 0:512], AF.Sigmoid))
                                assert c_a.n == a_sgx[cl]
                                if cl >= 1:
                                    pvb = 5 if (cl - 1) % 2 == 0 else 7
                                    sc.wait_ge(c_pe.sem, pe_vl[cl - 1])
                                    c_a.inc(sc.copy(
                                        vstage[:, :,
                                               (cl - 1) * 64:cl * 64],
                                        PB[pvb][:].rearrange(
                                            "p (h c) -> p h c", h=H)))
                                    assert c_a.n == a_vev[cl - 1]
                            sc.wait_ge(c_pe.sem, pe_vl[7])
                            c_a.inc(sc.copy(
                                vstage[:, :, 7 * 64:8 * 64],
                                PB[7][:].rearrange("p (h c) -> p h c", h=H)))
                            assert c_a.n == a_vev[7]
                        blk.scalar(_f_a)

                    if p1cut < 8:
                        continue
                    with nc.Block(no_gpsimd_drain=True) as blk:
                        def _gv1(sc):
                            for h in range(H):
                                ii = sc.activation(vsq[:, h, :],
                                                   vstage[:, h, :],
                                                   AF.Square)
                            c_a.inc(ii)
                        blk.scalar(_gv1)

                        def _gp1(t):
                            t.wait_ge(c_a.sem, c_a.n)
                            for h in range(H):
                                t.matmul(PB[0][0:8, :],
                                         ohc8_bf[:, h * 8:(h + 1) * 8],
                                         vsq[:, h, :],
                                         start=(h == 0), stop=(h == H - 1))
                        blk.tensor(_gp1)
                        dripg(blk)
                    with nc.Block(no_gpsimd_drain=True) as blk:
                        def _gv2(v):
                            def e(inst):
                                c_v.inc(inst); c_v.wait(v)
                            e(v.tensor_scalar(n8a, PB[0][0:8, :], 1.0,
                                              1e-6, ALU.mult, ALU.add))
                            _rsqrt(v, c_v, rstd8, n8a, n8b, n8c)
                            e(v.tensor_mul(comb8, rstd8, sig[:, 1, :]))
                        blk.vector(_gv2)
                        dripg(blk)
                    with nc.Block(no_gpsimd_drain=True) as blk:
                        v_g = c_v.n

                        def _gp2(t, v_g=v_g):
                            for h in range(H):
                                if h >= 2:
                                    t.wait_ge(c_v.sem, v_g + h - 1)
                                c_pe.inc(t.matmul(
                                    PB[1 + h % 2][:],
                                    oh8_f[:, h * 128:(h + 1) * 128],
                                    comb8, start=True, stop=True))
                        blk.tensor(_gp2)

                        pe_gb = c_pe.n - 8

                        def _gv3(v, pe_gb=pe_gb):
                            for h in range(H):
                                v.wait_ge(c_pe.sem, pe_gb + h + 1)
                                c_v.inc(v.scalar_tensor_tensor(
                                    vf[:, h, :], vstage[:, h, :],
                                    mhg1_f[:, h:h + 1], PB[1 + h % 2][:],
                                    ALU.mult, ALU.mult))
                        blk.vector(_gv3)
                        dripg(blk)
                    with nc.Block(no_gpsimd_drain=True) as blk:
                        a_cb = c_a.n

                        def _gp3(t, a_cb=a_cb):
                            for oc in range(KC):
                                if oc >= 2:
                                    t.wait_ge(c_a.sem, a_cb + oc - 1)
                                for h in range(H):
                                    ii = t.matmul(PB[2 + oc % 2][:],
                                                  wcomb_bf[:, h,
                                                           oc * 128:
                                                           (oc + 1) * 128],
                                                  vf[:, h, :],
                                                  start=(h == 0),
                                                  stop=(h == H - 1))
                                c_pe.inc(ii)
                        blk.tensor(_gp3)

                        pe_cb = c_pe.n - 8

                        def _ga1(sc, pe_cb=pe_cb):
                            for oc in range(KC):
                                sc.wait_ge(c_pe.sem, pe_cb + oc + 1)
                                c_a.inc(sc.copy(cstage[:, oc, :],
                                                PB[2 + oc % 2][:]))
                        blk.scalar(_ga1)

                        def _gs1(sy, ti=ti):
                            sy.wait_ge(c_a.sem, c_a.n)
                            wdt = TT if ti < NT - 1 else TT - 63
                            c_st.inc(sy.dma_start(
                                d_scr.rearrange("(k p) n -> p k n", p=128)
                                [:, :, 63 + ti * TT:63 + ti * TT + wdt],
                                cstage[:, :, 0:wdt]), 16)
                            sy.wait_ge(c_st.sem, c_st.n)
                        blk.sync(_gs1)
                        dripg(blk)

                    if debug and ti == 0:
                        with nc.Block(no_gpsimd_drain=True) as blk:
                            def _dbg(gp9):
                                for nm, src in [
                                    ("rstd0", rstd[0][:]), ("ss0", ssb[:]),
                                    ("sr0", srb[:]), ("k0", kb[:]),
                                    ("v0", vb[:]), ("q0", qb[:]),
                                    ("sig0", sig[:]), ("sg0", sgt[:]),
                                    ("gm64", gm64[:]),
                                    ("gtm0", gtm[:, 0, :, :]),
                                    ("ktm0", ktm[:, 0, :, :]),
                                    ("wc00", wcb[:, 1, 0, 0, :]),
                                    ("wc10", wcb[:, 1, 1, 0, :]),
                                    ("vst0", vstage[:]),
                                    ("comb0", cstage[:]),
                                ]:
                                    if nm in dbg:
                                        c_dbg.inc(gp9.dma_start(dbg[nm], src),
                                                  16)
                                gp9.wait_ge(c_dbg.sem, c_dbg.n)
                            blk.gpsimd(_dbg)
        else:
            pz = ExitStack()
            with pz:
                zt = pz.enter_context(
                    nc.sbuf_tensor("zt_stub", [128, KC, TT], BF16))
                with nc.Block(no_gpsimd_drain=True) as blk:
                    def _z1(v):
                        v.memset(zt[:], 0.0)
                    blk.vector(_z1)
                with nc.Block(no_gpsimd_drain=True) as blk:
                    def _z2(sy):
                        for ti in range(NT):
                            c_st.inc(sy.dma_start(
                                d_scr.rearrange("(k p) n -> p k n", p=128)
                                [:, :, ti * TT:(ti + 1) * TT], zt[:]), 16)
                        sy.wait_ge(c_st.sem, c_st.n)
                    blk.sync(_z2)

        if not attn:
            with nc.Block(no_gpsimd_drain=True) as blk:
                def _cva(gp):
                    for ti in range(16):
                        conv_chunk(gp, ti)
                blk.gpsimd(_cva)

        NGR = 24
        p2 = ExitStack()
        with p2:
            def s2(name, shape, dt=F32):
                return p2.enter_context(nc.sbuf_tensor(name, shape, dt))

            w1g = s2("w1g", [128, KC, DFF], BF16)
            xffn = w1g[:, :, 0:ST]
            xm_bf = w1g[:, :, ST:2 * ST]
            xm_bf2 = w1g[:, :, 2 * ST:3 * ST]
            hbuf = s2("hbuf", [128, 8192], BF16)
            xattn = hbuf.rearrange("p (a n) -> p a n", a=KC)
            ybf = xattn
            hv = hbuf.rearrange("p (f s) -> p f s", s=256)
            w2rot = s2("w2rot", [128, 3, 8, 1024], BF16)
            xattnB = w2rot[:, 0, :, :]
            ybfB = xattnB
            xmB = None
            xw512 = s2("xw512", [128, KC, 512])
            sq2 = s2("sq2", [128, KC, 512], BF16)
            muf = s2("muf", [128, 512]); s2f = s2("s2f", [128, 512])
            varf = s2("varf", [128, 512]); rstdf = s2("rstdf", [128, 512])
            exf = s2("exf", [128, 64]); mxn = s2("mxn", [128, 8])
            eqf = s2("eqf", [128, 64]); omf = s2("omf", [128, 64])
            prf = s2("prf", [128, 64]); fstf = s2("fstf", [128, 64])
            ssumf = s2("ssumf", [128, 8]); gatef = s2("gatef", [128, 8])
            zrow = s2("zrow", [128, 64])
            fstb = s2("fstb", [128, 32, 8], BF16)
            cnts = s2("cnts", [8, 32]); incl = s2("incl", [8, 32])
            offs = s2("offs", [8, 32]); dgall = s2("dgall", [8, 256])
            toti = s2("toti", [8, 1], I32); padi = s2("padi", [8, 1], I32)
            padf = s2("padf", [8, 1]); basec = s2("basec", [8, 1])
            gm24 = s2("gm24", [8, 24]); grpexp = s2("grpexp", [1, 24])
            dmat = s2("dmat", [128, 8]); destf = s2("destf", [128, 1])
            desti2 = s2("desti2", [128, 2], I32)
            oobt = s2("oobt", [128, 48], I32)
            tokid_sb = s2("tokid_sb", [128, 32], I32)
            dcol_sb = s2("dcol_sb", [128, 8])
            fcol_sb = s2("fcol_sb", [128, 32])
            slot_sb = s2("slot_sb", [8, 24])
            tri_bf = s2("tri_bf", [128, 128], BF16)
            tri8_sb = s2("tri8_sb", [8, 8])
            sidx3 = s2("sidx3", [128, 6], I32)
            gsl2 = s2("gsl2", [128, 4])
            egc = s2("egc", [128, 1]); eg8 = s2("eg8", [128, 1])
            ecs = s2("ecs", [128, 1])
            b8f = s2("b8f", [8, 1]); bi8 = s2("bi8", [8, 1], I32)
            widxf = s2("widxf", [128, 2, KC])
            widx = s2("widx", [128, 2, KC], I32)
            widx2f = s2("widx2f", [128, 2, 32])
            widx2 = s2("widx2", [128, 2, 32], I32)
            b1g8 = s2("b1g8", [8, 2, 512]); b2g8 = s2("b2g8", [8, 2, 128])
            b1fm = s2("b1fm", [128, 32]); b2fm = s2("b2fm", [128, 8])
            xg2 = s2("xg2", [128, 4, 1024], BF16)
            xg_fm = s2("xg_fm", [128, KC, 256], BF16)
            yg_fm = s2("yg_fm", [128, KC, 256], BF16)
            yg_tm = s2("yg_tm", [128, 2, 1024], BF16)

            with nc.Block(no_gpsimd_drain=True) as blk:
                def _pt(sc):
                    sc.preload_act_table(AF.Exp)
                blk.scalar(_pt)

                def _ptv(v):
                    v.memset(zrow[:], 0.0)
                    v.memset(offs[:], 0.0)
                    v.memset(oobt[:], 4096)
                    v.memset(xg2[:], 0.0)
                blk.vector(_ptv)

                def _pts(sy):
                    for dst, srcd in [(tokid_sb[:], d_tokid),
                                      (dcol_sb[:], d_dcol),
                                      (slot_sb[:], d_slot),
                                      (tri8_sb[:], d_tri8)]:
                        c_x.inc(sy.dma_start(dst, srcd), 16)
                    sy.wait_ge(c_x.sem, c_x.n)
                blk.sync(_pts)

                def _ptg(gp):
                    c_g.inc(gp.dma_start(tri_bf[:], d_tri), 16)
                    gp.wait_ge(c_g.sem, c_g.n)
                blk.gpsimd(_ptg)
            with nc.Block(no_gpsimd_drain=True) as blk:
                def _pre2g(gp):
                    c_g.inc(gp.dma_start(
                        d_sid.rearrange("(p a) c -> p (a c)", p=128),
                        oobt[:]), 16)
                    c_g.inc(gp.dma_start(d_xtm[4096:4224, :],
                                         xg2[:, 0, :]), 16)
                    c_g.inc(gp.dma_start(d_gat[4096:4224, :],
                                         zrow[:, 0:1]), 16)
                    gp.wait_ge(c_g.sem, c_g.n)
                blk.gpsimd(_pre2g)

                def _fc(v):
                    ii = None
                    for a in range(4):
                        ii = v.tensor_scalar(fcol_sb[:, a * 8:(a + 1) * 8],
                                             dcol_sb[:], 1.0,
                                             float(1024 * a),
                                             ALU.mult, ALU.add)
                    c_v.inc(ii)
                blk.vector(_fc)

            g_xa = {}
            if p2stage >= 1:
                with nc.Block(no_gpsimd_drain=True) as blk:
                    def _ga0(gp):
                        for kc in range(KC):
                            c_g.inc(gp.indirect_dma_start(
                                xattn[:, kc, :], None,
                                d_scr_rows,
                                IndirectOffsetOnAxis(
                                    gidx_sb[:, kc:kc + 1], 0),
                                element_offset=0), 16)
                        g_xa[0] = (c_g, c_g.n)
                    blk.gpsimd(_ga0)
            for st_i in range(NST):
                scol = st_i * ST
                xat = xattn if st_i % 2 == 0 else xattnB

                if p2stage < 1:
                    continue
                if st_i + 1 < NST:
                    with nc.Block(no_gpsimd_drain=True) as blk:
                        def _ga(gp, st_i=st_i):
                            dst = xattnB if (st_i + 1) % 2 == 1 else xattn
                            ch = (c_g, c_w2)[(st_i + 1) % 2]
                            for kc in range(KC):
                                ch.inc(gp.indirect_dma_start(
                                    dst[:, kc, :], None,
                                    d_scr_rows,
                                    IndirectOffsetOnAxis(
                                        gidx_sb[:, kc:kc + 1], 0),
                                    element_offset=(st_i + 1) * ST), 16)
                            g_xa[st_i + 1] = (ch, ch.n)
                        blk.gpsimd(_ga)

                def ln_pass(xin, xout, with_xwin, scol, gwait=None):
                    for bj in range(ST // 512):
                        b0 = bj * 512
                        with nc.Block(no_gpsimd_drain=True) as blk:
                            if with_xwin:
                                def _lw(sy, scol=scol, b0=b0):
                                    c_x.inc(sy.dma_start(
                                        xw512[:],
                                        d_xwin.rearrange(
                                            "(k p) n -> p k n", p=128)
                                        [:, :, scol + b0:scol + b0 + 512]),
                                        16)
                                    sy.wait_ge(c_x.sem, c_x.n)
                                blk.sync(_lw)

                            def _l1(sc, xin=xin, b0=b0, gwait=gwait,
                                    bj=bj):
                                if gwait is not None and bj == 0:
                                    sc.wait_ge(gwait[0].sem, gwait[1])
                                for kc in range(KC):
                                    ii = sc.activation(
                                        sq2[:, kc, :],
                                        xin[:, kc, b0:b0 + 512], AF.Square)
                                c_a.inc(ii)
                            blk.scalar(_l1)

                            def _l2(t, xin=xin, b0=b0):
                                t.wait_ge(c_a.sem, c_a.n)
                                for kc in range(KC):
                                    t.matmul(PB[0][:], ones_bf[:],
                                             xin[:, kc, b0:b0 + 512],
                                             start=(kc == 0),
                                             stop=(kc == KC - 1))
                                for kc in range(KC):
                                    t.matmul(PB[1][:], ones_bf[:],
                                             sq2[:, kc, :],
                                             start=(kc == 0),
                                             stop=(kc == KC - 1))
                            blk.tensor(_l2)
                        with nc.Block(no_gpsimd_drain=True) as blk:
                            def _l3(v):
                                def e(inst):
                                    c_v.inc(inst); c_v.wait(v)
                                e(v.tensor_scalar(muf[:], PB[0][:], 1.0 / D,
                                                  None, ALU.mult))
                                e(v.tensor_scalar(s2f[:], PB[1][:], 1.0 / D,
                                                  None, ALU.mult))
                                e(v.tensor_mul(varf[:], muf[:], muf[:]))
                                e(v.scalar_tensor_tensor(varf[:], varf[:],
                                                         -1.0, s2f[:],
                                                         ALU.mult, ALU.add))
                                e(v.tensor_scalar(varf[:], varf[:], 1.0,
                                                  1e-5, ALU.mult, ALU.add))
                                _rsqrt(v, c_v, rstdf[:], varf[:], s2f[:],
                                       varf[:])
                            blk.vector(_l3)
                        with nc.Block(no_gpsimd_drain=True) as blk:
                            def _l4(v, xin=xin, xout=xout, b0=b0,
                                    with_xwin=with_xwin):
                                for kc in range(KC):
                                    ii = v.tensor_sub(sq2[:, kc, :],
                                                      xin[:, kc, b0:b0 + 512],
                                                      muf[:])
                                c_v.inc(ii); c_v.wait(v)
                                for kc in range(KC):
                                    if with_xwin:
                                        ii = v.tensor_mul(sq2[:, kc, :],
                                                          sq2[:, kc, :],
                                                          rstdf[:])
                                    else:
                                        ii = v.tensor_mul(
                                            xout[:, kc, b0:b0 + 512],
                                            sq2[:, kc, :], rstdf[:])
                                c_v.inc(ii); c_v.wait(v)
                                if with_xwin:
                                    for kc in range(KC):
                                        v.tensor_add(xout[:, kc, b0:b0 + 512],
                                                     sq2[:, kc, :],
                                                     xw512[:, kc, :])
                            blk.vector(_l4)

                if p2stage < 2:
                    continue
                ln_pass(xat, xm_bf, True, scol, gwait=g_xa[st_i])
                ln_pass(xm_bf, xffn, False, scol)

                if p2stage < 3:
                    continue
                with nc.Block(no_gpsimd_drain=True) as blk:
                    def _r1(t):
                        for sub in range(8):
                            for kc in range(KC):
                                t.matmul(PB[0][:, sub * 8:sub * 8 + 8],
                                         xffn[:, kc,
                                              sub * 128:(sub + 1) * 128],
                                         rw_bf[:, kc, :],
                                         start=(kc == 0), stop=False)
                            ii = t.matmul(PB[0][:, sub * 8:sub * 8 + 8],
                                          ones_bf[0:1, :], rb_bf[:],
                                          start=False, stop=True)
                        c_pe.inc(ii)
                    blk.tensor(_r1)

                    if rcut < 1:
                        continue

                    def _r2(v):
                        v.wait_ge(c_pe.sem, c_pe.n)
                        for sub in range(8):
                            ii = v.reduce_max(mxn[:, sub:sub + 1],
                                              PB[0][:, sub * 8:sub * 8 + 8],
                                              AX.X, negate=True)
                        c_v.inc(ii); c_v.wait(v)
                        for sub in range(8):
                            ii = v.tensor_scalar(
                                eqf[:, sub * 8:sub * 8 + 8],
                                PB[0][:, sub * 8:sub * 8 + 8],
                                mxn[:, sub:sub + 1], 0.0,
                                ALU.add, ALU.is_ge)
                        c_v.inc(ii)
                    blk.vector(_r2)

                    if rcut < 2:
                        continue

                    def _r4(sc):
                        sc.wait_ge(c_v.sem, c_v.n)
                        for sub in range(8):
                            c_a.inc(sc.activation(
                                exf[:, sub * 8:sub * 8 + 8],
                                PB[0][:, sub * 8:sub * 8 + 8], AF.Exp,
                                bias=mxn[:, sub:sub + 1]))
                    blk.scalar(_r4)
                if rcut < 3:
                    continue
                with nc.Block(no_gpsimd_drain=True) as blk:
                    def _r5(v, st_i=st_i):
                        for sub in range(8):
                            ii = v.tensor_scalar(omf[:, sub * 8:sub * 8 + 8],
                                                 eqf[:, sub * 8:sub * 8 + 8],
                                                 -1.0, 1.0, ALU.mult,
                                                 ALU.add)
                        c_v.inc(ii)
                        c_v.wait(v)
                        for sub in range(8):
                            v.reduce_sum(ssumf[:, sub:sub + 1],
                                         exf[:, sub * 8:sub * 8 + 8], AX.X)
                        for sub in range(8):
                            ii = v.tensor_tensor_scan(
                                prf[:, sub * 8:sub * 8 + 8],
                                omf[:, sub * 8:sub * 8 + 8],
                                zrow[:, sub * 8:sub * 8 + 8],
                                1.0, ALU.mult, ALU.add)
                        c_v.inc(ii)
                        c_v.wait(v)
                        v.reciprocal(gatef[:], ssumf[:])
                        for sub in range(8):
                            v.tensor_copy(fstf[:, sub * 8:sub * 8 + 1],
                                          eqf[:, sub * 8:sub * 8 + 1])
                        for sub in range(8):
                            ii = v.tensor_mul(fstf[:, sub * 8 + 1:sub * 8 + 8],
                                              eqf[:, sub * 8 + 1:sub * 8 + 8],
                                              prf[:, sub * 8:sub * 8 + 7])
                        c_v.inc(ii)
                        c_v.wait(v)
                        for sub in range(8):
                            ii = v.tensor_copy(fstb[:, st_i * 8 + sub, :],
                                               fstf[:, sub * 8:sub * 8 + 8])
                        c_v.inc(ii)
                    blk.vector(_r5)

                    def _r6(sy, st_i=st_i):
                        sy.wait_ge(c_v.sem, c_v.n)
                        for sub in range(8):
                            c_st.inc(sy.dma_start(
                                d_gat.rearrange("(a p) c -> p a c", p=128)
                                [:, st_i * 8 + sub, :],
                                gatef[:, sub:sub + 1]), 16)
                        sy.wait_ge(c_st.sem, c_st.n)
                    blk.sync(_r6)

                    def _r7(t, st_i=st_i):
                        t.wait_ge(c_v.sem, c_v.n)
                        ii = None
                        for sub in range(8):
                            ii = t.matmul(PB[3][0:8, sub:sub + 1],
                                          fstb[:, st_i * 8 + sub, :],
                                          ones_bf[:, 0:1],
                                          start=True, stop=True)
                        c_pe.inc(ii)
                    blk.tensor(_r7)
                with nc.Block(no_gpsimd_drain=True) as blk:
                    def _r8(v, st_i=st_i):
                        c_v.inc(v.tensor_copy(cnts[:, st_i * 8:st_i * 8 + 8],
                                              PB[3][0:8, 0:8]))
                    blk.vector(_r8)

                if p2stage < 4:
                    continue
                peT = c_pe.n
                vbT = c_v.n
                stT = [c_st.n, c_st2.n]
                with nc.Block(no_gpsimd_drain=True) as blk:
                    def _t1(t, peT=peT, vbT=vbT):
                        for tc in range(16):
                            if tc >= 2:
                                t.wait_ge(c_v.sem, vbT + tc - 1)
                            bk = 2 + 2 * (tc % 2)
                            src = xffn if tc < 8 else xm_bf
                            co = (tc % 8) * 128
                            ii = None
                            for kc in range(KC):
                                ii = t.transpose(
                                    PBbf(bk)[:, kc * 128:(kc + 1) * 128],
                                    src[:, kc, co:co + 128],
                                    id_bf[:])
                            c_pe.inc(ii)
                    blk.tensor(_t1)

                    def _t2(v, peT=peT, stT=stT):
                        for tc in range(16):
                            v.wait_ge(c_pe.sem, peT + tc + 1)
                            if tc >= 2:
                                ch = (c_st, c_st2)[tc % 2]
                                v.wait_ge(ch.sem,
                                          stT[tc % 2] + 16 * (tc // 2))
                            bk = 2 + 2 * (tc % 2)
                            c_v.inc(v.tensor_copy(xg2[:, tc % 2, :],
                                                  PBbf(bk)[:, 0:1024]))
                    blk.vector(_t2)

                    def _t3(sy, st_i=st_i, vbT=vbT):
                        for tc in range(16):
                            sy.wait_ge(c_v.sem, vbT + tc + 1)
                            ch = (c_st, c_st2)[tc % 2]
                            if tc < 8:
                                dst = d_xtm.rearrange(
                                    "(a p) d -> p a d", p=128)[
                                    :, st_i * 8 + tc, :]
                            else:
                                dst = d_xmt.rearrange(
                                    "(a p) d -> p a d", p=128)[
                                    :, st_i * 8 + (tc - 8), :]
                            ch.inc(sy.dma_start(dst, xg2[:, tc % 2, :]), 16)
                        sy.wait_ge(c_st.sem, c_st.n)
                        sy.wait_ge(c_st2.sem, c_st2.n)
                    blk.sync(_t3)

            if scut >= 2 and p2stage >= 4:
                with nc.Block(no_gpsimd_drain=True) as blk:
                    def _s1(v):
                        def e(inst):
                            c_v.inc(inst); c_v.wait(v)
                        e(v.tensor_tensor_scan(incl[:], cnts[:],
                                               zrow[0:8, 0:32], 0.0,
                                               ALU.add, ALU.add))
                        e(v.tensor_copy(toti[:], incl[:, 31:32]))
                        e(v.tensor_scalar(padi[:], toti[:], 255, None,
                                          ALU.bitwise_and))
                        e(v.tensor_scalar(padi[:], padi[:], -1, 256,
                                          ALU.mult, ALU.add))
                        e(v.tensor_scalar(padi[:], padi[:], 255, None,
                                          ALU.bitwise_and))
                        e(v.tensor_tensor(padi[:], toti[:], padi[:],
                                          ALU.add))
                        e(v.tensor_copy(padf[:], padi[:]))
                    blk.vector(_s1)
                with nc.Block(no_gpsimd_drain=True) as blk:
                    def _s2t(t):
                        c_pe.inc(t.matmul(PB[0][0:8, 0:1], tri8_sb[:],
                                          padf[:], start=True, stop=True))
                    blk.tensor(_s2t)

                    def _s2v(v):
                        v.wait_ge(c_pe.sem, c_pe.n)
                        def e(inst):
                            c_v.inc(inst); c_v.wait(v)
                        e(v.tensor_copy(basec[:], PB[0][0:8, 0:1]))
                        e(v.tensor_copy(offs[:, 1:32], incl[:, 0:31]))
                        e(v.tensor_scalar(offs[:], offs[:], basec[:, 0:1],
                                          None, ALU.add))
                        e(v.tensor_scalar(gm24[:], slot_sb[:],
                                          basec[:, 0:1], 0.0,
                                          ALU.subtract, ALU.is_ge))
                        ii = None
                        for c in range(32):
                            ii = v.tensor_scalar(dgall[:, c * 8:(c + 1) * 8],
                                                 id_f[0:8, 0:8],
                                                 offs[:, c:c + 1],
                                                 None, ALU.mult)
                        c_v.inc(ii); c_v.wait(v)
                    blk.vector(_s2v)
                with nc.Block(no_gpsimd_drain=True) as blk:
                    def _s3t(t):
                        c_pe.inc(t.matmul(PB[1][0:1, 0:24], ones_f[0:8, 0:1],
                                          gm24[:], start=True, stop=True))
                    blk.tensor(_s3t)

                    def _s3v(v):
                        v.wait_ge(c_pe.sem, c_pe.n)
                        c_v.inc(v.tensor_scalar(grpexp[:], PB[1][0:1, 0:24],
                                                1.0, -1.0, ALU.mult,
                                                ALU.add))
                        c_v.wait(v)
                    blk.vector(_s3v)
                pebS = c_pe.n
                vbS = c_v.n
                gbS = [c_g.n, c_w2.n]
                with nc.Block(no_gpsimd_drain=True) as blk:
                    def _s4t(t, pebS=pebS, vbS=vbS):
                        for c in range(32):
                            if c >= 2:
                                t.wait_ge(c_v.sem, vbS + 5 * (c - 2) + 5)
                            c_pe.inc(t.matmul(PB[c % 2][:, 0:8], tri_bf[:],
                                              fstb[:, c, :],
                                              start=True, stop=True))
                            c_pe.inc(t.matmul(PB[2 + c % 2][:, 0:8],
                                              ones_f[0:8, :],
                                              dgall[0:8, c * 8:(c + 1) * 8],
                                              start=True, stop=True))
                    blk.tensor(_s4t)

                    def _s4v(v, pebS=pebS, gbS=gbS):
                        for c in range(32):
                            v.wait_ge(c_pe.sem, pebS + 2 * c + 2)
                            if c >= 2:
                                ch = (c_g, c_w2)[c % 2]
                                v.wait_ge(ch.sem,
                                          gbS[c % 2] + 16 * (c // 2))
                            def e(inst):
                                c_v.inc(inst); c_v.wait(v)
                            e(v.tensor_copy(dmat[:], PB[c % 2][:, 0:8]))
                            e(v.tensor_tensor(dmat[:], dmat[:],
                                              PB[2 + c % 2][:, 0:8],
                                              ALU.add))
                            e(v.tensor_tensor(dmat[:], dmat[:],
                                              fstb[:, c, :], ALU.mult))
                            e(v.reduce_sum(destf[:], dmat[:], AX.X))
                            e(v.tensor_copy(desti2[:, c % 2:c % 2 + 1],
                                            destf[:]))
                    blk.vector(_s4v)

                    def _s4g(gp, vbS=vbS):
                        for c in range(32):
                            gp.wait_ge(c_v.sem, vbS + 5 * c + 5)
                            ch = (c_g, c_w2)[c % 2]
                            ch.inc(gp.indirect_dma_start(
                                d_sid, IndirectOffsetOnAxis(
                                    desti2[:, c % 2:c % 2 + 1], 0),
                                tokid_sb[:, c:c + 1], None,
                                oob_is_err=False), 16)
                        gp.wait_ge(c_g.sem, c_g.n)
                        gp.wait_ge(c_w2.sem, c_w2.n)
                    blk.gpsimd(_s4g)

            if scut >= 3 and p2stage >= 4:
                def _mark(d, k, ch):
                    if k in d:
                        assert d[k] == ch.n, (k, d[k], ch.n)
                    else:
                        d[k] = ch.n

                g_sm = {}; g_scat = {}
                w1A = {}; w1B = {}; w2m = {}
                x_sidx = {}
                v_widx = {}

                def smch(g):
                    return (c_g, c_w2)[g % 2]

                def w2ch(S):
                    return (c_w2a, c_w2b, c_w2c)[S % 3]

                def smalls(gp, g):
                    sl3 = (g % 3) * 2
                    sl2 = (g % 2) * 2
                    ch = smch(g)
                    for a in range(2):
                        ch.inc(gp.indirect_dma_start(
                            xg2[:, sl2 + a, :], None, d_xtm,
                            IndirectOffsetOnAxis(
                                sidx3[:, sl3 + a:sl3 + a + 1], 0),
                            oob_is_err=False), 16)
                        ch.inc(gp.indirect_dma_start(
                            gsl2[:, sl2 + a:sl2 + a + 1], None, d_gat,
                            IndirectOffsetOnAxis(
                                sidx3[:, sl3 + a:sl3 + a + 1], 0),
                            oob_is_err=False), 16)
                    ch.inc(gp.indirect_dma_start(
                        b1g8[0:8, g % 2, :], None,
                        d_b1r.rearrange("e (a q) -> (e a) q", a=8),
                        IndirectOffsetOnAxis(bi8[:, 0:1], 0),
                        oob_is_err=False), 16)
                    ch.inc(gp.indirect_dma_start(
                        b2g8[0:8, g % 2, :], None,
                        d_b2r.rearrange("e (a q) -> (e a) q", a=8),
                        IndirectOffsetOnAxis(bi8[:, 0:1], 0),
                        oob_is_err=False), 16)
                    _mark(g_sm, g, ch)

                def w1gather(gp, g, half):
                    off = half * 2048
                    ch = (c_w, c_w1)[half]
                    for kc in range(KC):
                        ch.inc(gp.indirect_dma_start(
                            w1g[:, kc, off:off + 2048], None, d_w1b,
                            IndirectOffsetOnAxis(
                                widx[:, g % 2, kc:kc + 1], 0),
                            element_offset=off), 16)
                    _mark(w1A if half == 0 else w1B, g, ch)

                def w2gather(gp, S):
                    gg, s = divmod(S, 4)
                    sl = S % 3
                    ch = w2ch(S)
                    for jj in range(8):
                        j = s * 8 + jj
                        ch.inc(gp.indirect_dma_start(
                            w2rot[:, sl, jj, :], None, d_w2b,
                            IndirectOffsetOnAxis(
                                widx2[:, gg % 2, j:j + 1], 0)), 16)
                    _mark(w2m, S, ch)

                def scat_y(gp, g):
                    sl3p = (g % 3) * 2
                    ch = c_scl[g % 3]
                    for a in range(2):
                        ch.inc(gp.indirect_dma_start(
                            d_ytm, IndirectOffsetOnAxis(
                                sidx3[:, sl3p + a:sl3p + a + 1], 0),
                            yg_tm[:, a, :], None,
                            oob_is_err=False), 16)
                        if a == 0:
                            gp.wait_ge(ch.sem, ch.n)
                    _mark(g_scat, g, ch)

                def widx_chain(v, g):
                    sl = g % 2
                    def e(inst):
                        c_v.inc(inst); c_v.wait(v)
                    e(v.tensor_copy(egc[:], PB[2][:, 0:1]))
                    e(v.tensor_scalar(eg8[:], egc[:], 8.0, None, ALU.mult))
                    e(v.tensor_scalar(b8f[:], dcol_sb[0:8, 0:1],
                                      eg8[0:8, 0:1], None, ALU.add))
                    e(v.tensor_copy(bi8[:], b8f[:]))
                    e(v.tensor_scalar(ecs[:], egc[:], 1024.0, None,
                                      ALU.mult))
                    e(v.tensor_scalar(widxf[:, sl, :], dcol_sb[:],
                                      ecs[:, 0:1], None, ALU.add))
                    e(v.tensor_copy(widx[:, sl, :], widxf[:, sl, :]))
                    e(v.tensor_scalar(ecs[:], egc[:], 4096.0, None,
                                      ALU.mult))
                    e(v.tensor_scalar(widx2f[:, sl, :], fcol_sb[:],
                                      ecs[:, 0:1], None, ALU.add))
                    e(v.tensor_copy(widx2[:, sl, :], widx2f[:, sl, :]))
                    _mark(v_widx, g, c_v)

                with nc.Block(no_gpsimd_drain=True) as blk:
                    def _p3t(t):
                        c_pe.inc(t.matmul(PB[2][:, 0:1], ones_f[0:1, :],
                                          grpexp[0:1, 0:1], start=True,
                                          stop=True))
                    blk.tensor(_p3t)

                    def _p3s(sy):
                        for a in range(2):
                            c_sx[0].inc(sy.dma_start(
                                sidx3[:, a:a + 1],
                                d_sid.rearrange("(b p) c -> p b c", p=128)
                                [:, a, :]), 16)
                        _mark(x_sidx, 0, c_sx[0])
                        sy.wait_ge(c_sx[0].sem, c_sx[0].n)
                    blk.sync(_p3s)

                    def _p3v(v):
                        v.wait_ge(c_pe.sem, c_pe.n)
                        widx_chain(v, 0)
                    blk.vector(_p3v)
                with nc.Block(no_gpsimd_drain=True) as blk:
                    def _p3g(gp):
                        gp.wait_ge(c_wc.sem, c_wc.n)
                        smalls(gp, 0)
                        w1gather(gp, 0, 0)
                        w1gather(gp, 0, 1)
                        w2gather(gp, 0)
                        w2gather(gp, 1)
                        w2gather(gp, 2)
                    blk.gpsimd(_p3g)

                pe_inT = {}; pe_eg = {}; pe_bT = {}
                pe_p1 = {}; pe_p2 = {}; pe_outT = {}
                cc = c_pe.n
                for g in range(NGR):
                    cc += 1; pe_inT[g] = cc
                    if g + 1 < NGR:
                        cc += 1; pe_eg[g + 1] = cc
                    cc += 1; pe_bT[g] = cc
                    for fi in range(32):
                        cc += 1; pe_p1[(g, fi)] = cc
                    for j in range(32):
                        cc += 1; pe_p2[(g, j)] = cc
                    cc += 1; pe_outT[g] = cc
                pe_end = cc

                v_xfm = {}; v_bias = {}; v_yfm = {}; v_ytm = {}
                cc = c_v.n
                for g in range(NGR):
                    cc += 2; v_xfm[g] = cc
                    if g + 1 < NGR:
                        cc += 10; v_widx[g + 1] = cc
                    cc += 2; v_bias[g] = cc
                    cc += 8; v_yfm[g] = cc
                    cc += 2; v_ytm[g] = cc
                v_end = cc

                a_b3 = c_a.n

                def a_h3(g, fi):
                    return a_b3 + 32 * g + fi + 1
                a_end = a_b3 + 32 * NGR

                xbase = [c.n for c in c_sx]
                per3 = [0, 0, 0]
                for g in range(1, NGR):
                    per3[g % 3] += 32
                    x_sidx[g] = xbase[g % 3] + per3[g % 3]

                with nc.Block(no_gpsimd_drain=True) as blk:
                    def _m_g(gp):
                        for g in range(NGR):
                            if g >= 1:
                                gp.wait_ge(c_v.sem, v_ytm[g - 1])
                                scat_y(gp, g - 1)
                            if g + 1 < NGR:
                                gp.wait_ge(c_sx[(g + 1) % 3].sem,
                                           x_sidx[g + 1])
                                gp.wait_ge(c_v.sem, v_widx[g + 1])
                                gp.wait_ge(c_pe.sem, pe_bT[g])
                                smalls(gp, g + 1)
                                gp.wait_ge(c_pe.sem, pe_p1[(g, 15)])
                                w1gather(gp, g + 1, 0)
                                gp.wait_ge(c_pe.sem, pe_p1[(g, 31)])
                                w1gather(gp, g + 1, 1)
                            for S in range(4 * g + 3, 4 * g + 7):
                                if S >= 4 * NGR:
                                    continue
                                gs2, ss2 = divmod(S - 3, 4)
                                gp.wait_ge(c_pe.sem,
                                           pe_p2[(gs2, ss2 * 8 + 7)])
                                w2gather(gp, S)
                        gp.wait_ge(c_v.sem, v_ytm[NGR - 1])
                        scat_y(gp, NGR - 1)
                        for ch in c_scl:
                            gp.wait_ge(ch.sem, ch.n)
                        gp.wait_ge(c_g.sem, c_g.n)
                        gp.wait_ge(c_w2.sem, c_w2.n)
                        gp.wait_ge(c_w.sem, c_w.n)
                        gp.wait_ge(c_w1.sem, c_w1.n)
                        gp.wait_ge(c_w2a.sem, c_w2a.n)
                        gp.wait_ge(c_w2b.sem, c_w2b.n)
                        gp.wait_ge(c_w2c.sem, c_w2c.n)
                    blk.gpsimd(_m_g)

                    def _m_s(sy):
                        for g in range(1, NGR):
                            if g >= 3:
                                sy.wait_ge(c_scl[g % 3].sem, g_scat[g - 3])
                            sl3 = (g % 3) * 2
                            for a in range(2):
                                c_sx[g % 3].inc(sy.dma_start(
                                    sidx3[:, sl3 + a:sl3 + a + 1],
                                    d_sid.rearrange("(b p) c -> p b c",
                                                    p=128)
                                    [:, 2 * g + a, :]), 16)
                            _mark(x_sidx, g, c_sx[g % 3])
                        for cs in c_sx:
                            sy.wait_ge(cs.sem, cs.n)
                    blk.sync(_m_s)

                    def _m_pe(t):
                        for g in range(NGR):
                            if g >= 1:
                                t.wait_ge(c_v.sem, v_ytm[g - 1])
                            else:
                                t.wait_ge(c_v.sem, v_widx[0])
                            t.wait_ge(smch(g).sem, g_sm[g])
                            sl2 = (g % 2) * 2
                            ii = None
                            for a in range(2):
                                for kc in range(KC):
                                    ii = t.transpose(
                                        PBbf(6 + a)[:, kc * 128:
                                                    (kc + 1) * 128],
                                        xg2[:, sl2 + a,
                                            kc * 128:(kc + 1) * 128],
                                        id_bf[:])
                            c_pe.inc(ii)
                            assert c_pe.n == pe_inT[g]
                            if g + 1 < NGR:
                                c_pe.inc(t.matmul(
                                    PB[2][:, 0:1], ones_f[0:1, :],
                                    grpexp[0:1, g + 1:g + 2],
                                    start=True, stop=True))
                                assert c_pe.n == pe_eg[g + 1]
                            for a4 in range(4):
                                t.transpose(PB[3][:, a4 * 8:(a4 + 1) * 8],
                                            b1g8[0:8, g % 2,
                                                 a4 * 128:(a4 + 1) * 128],
                                            id_f[0:8, 0:8])
                            ii = t.transpose(PB[3][:, 32:40],
                                             b2g8[0:8, g % 2, :],
                                             id_f[0:8, 0:8])
                            c_pe.inc(ii)
                            assert c_pe.n == pe_bT[g]
                            for fi in range(32):
                                if fi == 0:
                                    t.wait_ge(c_w.sem, w1A[g])
                                    t.wait_ge(c_v.sem, v_xfm[g])
                                if fi == 16:
                                    t.wait_ge(c_w1.sem, w1B[g])
                                if fi >= 2:
                                    t.wait_ge(c_a.sem, a_h3(g, fi - 2))
                                elif g >= 1:
                                    t.wait_ge(c_a.sem, a_h3(g - 1, 30 + fi))
                                for kc in range(KC):
                                    ii = t.matmul(
                                        PB[fi % 2][:, 0:256],
                                        w1g[:, kc,
                                            fi * 128:(fi + 1) * 128],
                                        xg_fm[:, kc, :],
                                        start=(kc == 0),
                                        stop=(kc == KC - 1))
                                c_pe.inc(ii)
                                assert c_pe.n == pe_p1[(g, fi)]
                            t.wait_ge(c_v.sem, v_bias[g])
                            t.wait_ge(c_a.sem, a_h3(g, 31))
                            for j in range(32):
                                s = j // 8
                                if j % 8 == 0:
                                    t.wait_ge(w2ch(4 * g + s).sem,
                                              w2m[4 * g + s])
                                for o in range(8):
                                    if o == 0:
                                        t.wait_ge(c_a.sem, a_h3(g, j))
                                    ii = t.matmul(
                                        PB[o][:, 0:256],
                                        w2rot[:, (4 * g + s) % 3, j % 8,
                                              o * 128:(o + 1) * 128],
                                        hv[:, j, :],
                                        start=(j == 0), stop=(j == 31))
                                c_pe.inc(ii)
                                assert c_pe.n == pe_p2[(g, j)]
                            t.wait_ge(c_v.sem, v_yfm[g])
                            for a in range(2):
                                for oc in range(8):
                                    ii = t.transpose(
                                        PBbf(7 - a)[:, oc * 128:
                                                    (oc + 1) * 128],
                                        yg_fm[:, oc, a * 128:(a + 1) * 128],
                                        id_bf[:])
                            c_pe.inc(ii)
                            assert c_pe.n == pe_outT[g]
                        assert c_pe.n == pe_end
                    blk.tensor(_m_pe)

                    def _m_v(v):
                        for g in range(NGR):
                            v.wait_ge(c_pe.sem, pe_inT[g])
                            c_v.inc(v.tensor_copy(
                                xg_fm[:, :, 0:128],
                                PBbf(6).rearrange("p (k s) -> p k s",
                                                  s=128)))
                            c_v.inc(v.tensor_copy(
                                xg_fm[:, :, 128:256],
                                PBbf(7).rearrange("p (k s) -> p k s",
                                                  s=128)))
                            assert c_v.n == v_xfm[g]
                            if g + 1 < NGR:
                                v.wait_ge(c_pe.sem, pe_eg[g + 1])
                                v.wait_ge(smch(g).sem, g_sm[g])
                                widx_chain(v, g + 1)
                                assert c_v.n == v_widx[g + 1]
                            v.wait_ge(c_pe.sem, pe_bT[g])
                            c_v.inc(v.tensor_copy(b1fm[:], PB[3][:, 0:32]))
                            c_v.inc(v.tensor_copy(b2fm[:], PB[3][:, 32:40]))
                            assert c_v.n == v_bias[g]
                            v.wait_ge(c_pe.sem, pe_p2[(g, 31)])
                            v.wait_ge(c_v.sem, v_bias[g])
                            for oc in range(8):
                                c_v.inc(v.tensor_scalar(
                                    yg_fm[:, oc, :],
                                    PB[oc][:, 0:256],
                                    b2fm[:, oc:oc + 1], None, ALU.add))
                            assert c_v.n == v_yfm[g]
                            v.wait_ge(c_pe.sem, pe_outT[g])
                            if g >= 1:
                                v.wait_ge(c_scl[(g - 1) % 3].sem,
                                          g_scat[g - 1])
                            sl2 = (g % 2) * 2
                            for a in range(2):
                                c_v.inc(v.tensor_scalar(
                                    yg_tm[:, a, :],
                                    PBbf(7 - a)[:, 0:1024],
                                    gsl2[:, sl2 + a:sl2 + a + 1],
                                    None, ALU.mult))
                            assert c_v.n == v_ytm[g]
                        assert c_v.n == v_end
                    blk.vector(_m_v)

                    def _m_a(sc):
                        for g in range(NGR):
                            for fi in range(32):
                                if fi == 0:
                                    sc.wait_ge(c_v.sem, v_bias[g])
                                sc.wait_ge(c_pe.sem, pe_p1[(g, fi)])
                                bcol = (fi % 4) * 8 + fi // 4
                                c_a.inc(sc.activation(
                                    hv[:, fi, :], PB[fi % 2][:, 0:256],
                                    AF.Relu,
                                    bias=b1fm[:, bcol:bcol + 1]))
                                assert c_a.n == a_h3(g, fi)
                        assert c_a.n == a_end
                    blk.scalar(_m_a)

            if debug:
                with nc.Block(no_gpsimd_drain=True) as blk:
                    def _dbg2(gp9):
                        for nm, src in [("cnts", cnts[:]), ("offs", offs[:]),
                                        ("grpexp", grpexp[:]),
                                        ("sid", d_sid),
                                        ("gat", d_gat[0:4096, :]),
                                        ("xtm", d_xtm[0:4096, :]),
                                        ("ytm", d_ytm[0:4096, :])]:
                            if nm in dbg:
                                c_dbg.inc(gp9.dma_start(dbg[nm], src), 16)
                        gp9.wait_ge(c_dbg.sem, c_dbg.n)
                    blk.gpsimd(_dbg2)

            if scut >= 4 and p2stage >= 4:
                _w1f = w1g[:].rearrange("p k f -> p (k f)")
                yS = [_w1f[:, 0:4096], _w1f[:, 4096:8192]]
                xmS = [_w1f[:, 8192:12288], _w1f[:, 12288:16384]]
                sqS = _w1f[:, 16384:20480]
                oS = [hbuf[:].bitcast(F32),
                      _w1f[:, 20480:28672].bitcast(F32)]

                def v3(ap):
                    return ap.rearrange("p (a d) -> p a d", a=4)
                mu4 = muf[:, 0:4]; sm4 = muf[:, 4:8]
                var4 = varf[:, 0:4]; rst4 = rstdf[:, 0:4]
                t4a = s2f[:, 0:4]; t4b = s2f[:, 4:8]
                xl = {}; gl = {}; so = {}; vdone = {}; sqm = {}
                NSB = 8

                def yview(S):
                    return d_ytm[S * 512:(S + 1) * 512, :].rearrange(
                        "(a p) d -> p a d", p=128)

                def xmview(S):
                    return d_xmt[S * 512:(S + 1) * 512, :].rearrange(
                        "(a p) d -> p a d", p=128)

                with nc.Block(no_gpsimd_drain=True) as blk:
                    def _q0(sy):
                        for chn in c_scl:
                            sy.wait_ge(chn.sem, chn.n)
                        for b in range(2):
                            xl[b] = c_x.inc(
                                sy.dma_start(v3(yS[b]), yview(b)), 16)
                    blk.sync(_q0)

                    def _q0g(gp):
                        for chn in c_scl:
                            gp.wait_ge(chn.sem, chn.n)
                        for b in range(2):
                            gl[b] = c_g.inc(
                                gp.dma_start(v3(xmS[b]), xmview(b)), 16)
                    blk.gpsimd(_q0g)
                for b in range(NSB):
                    sl = b % 2
                    with nc.Block(no_gpsimd_drain=True) as blk:
                        def _c1(sc, b=b, sl=sl):
                            sc.wait_ge(c_x.sem, xl[b])
                            if b >= 1:
                                sc.wait_ge(c_v.sem, sqm[b - 1])
                            c_a.inc(sc.activation(sqS, yS[sl], AF.Square))
                        blk.scalar(_c1)
                        amark = c_a.n

                        def _c2(v, b=b, sl=sl, amark=amark):
                            def e(inst):
                                c_v.inc(inst); c_v.wait(v)
                            v.wait_ge(c_a.sem, amark)
                            e(v.tensor_reduce(sm4, v3(sqS), AX.X, ALU.add))
                            sqm[b] = c_v.n
                            e(v.tensor_reduce(mu4, v3(yS[sl]), AX.X,
                                              ALU.add))
                            e(v.tensor_scalar(mu4, mu4, 1.0 / D, None,
                                              ALU.mult))
                            e(v.tensor_mul(var4, mu4, mu4))
                            e(v.tensor_scalar(sm4, sm4, 1.0 / D, None,
                                              ALU.mult))
                            e(v.tensor_sub(var4, sm4, var4))
                            e(v.tensor_scalar(var4, var4, 1.0, 1e-5,
                                              ALU.mult, ALU.add))
                            _rsqrt(v, c_v, rst4, var4, t4a, t4b)
                            if b >= 2:
                                v.wait_ge(c_out.sem, so[b - 2])
                            mubc = mu4.unsqueeze(2).to_broadcast(
                                [128, 4, 1024])
                            rstbc = rst4.unsqueeze(2).to_broadcast(
                                [128, 4, 1024])
                            e(v.tensor_sub(v3(oS[sl]), v3(yS[sl]), mubc))
                            e(v.tensor_mul(v3(oS[sl]), v3(oS[sl]), rstbc))
                            v.wait_ge(c_g.sem, gl[b])
                            c_v.inc(v.tensor_add(v3(oS[sl]), v3(oS[sl]),
                                                 v3(xmS[sl])))
                            vdone[b] = c_v.n
                        blk.vector(_c2)

                        def _c3(sy, b=b, sl=sl):
                            sy.wait_ge(c_v.sem, vdone[b])
                            so[b] = c_out.inc(sy.dma_start(
                                d_out[b * 512:(b + 1) * 512, :].rearrange(
                                    "(a p) d -> p a d", p=128),
                                v3(oS[sl])), 16)
                            if b + 2 < NSB:
                                nb = b + 2
                                xl[nb] = c_x.inc(sy.dma_start(
                                    v3(yS[sl]), yview(nb)), 16)
                        blk.sync(_c3)

                        def _c3g(gp, b=b, sl=sl):
                            if b + 2 < NSB:
                                gp.wait_ge(c_v.sem, vdone[b])
                                nb = b + 2
                                gl[nb] = c_g.inc(gp.dma_start(
                                    v3(xmS[sl]), xmview(nb)), 16)
                        blk.gpsimd(_c3g)

        with nc.Block(no_gpsimd_drain=True) as blk:
            def _fin(sy):
                sy.wait_ge(c_out.sem, c_out.n)
            blk.sync(_fin)

    return nc


def prep_inputs(inputs):
    f = np.float32
    seq = np.asarray(inputs["L0_attn_seq"], f)
    wkv = np.ascontiguousarray(np.asarray(inputs["L0_attn_w_kv"], f))
    wq = np.ascontiguousarray(np.asarray(inputs["L0_attn_w_q"], f))
    wcomb = np.ascontiguousarray(np.asarray(inputs["L0_attn_w_combine"], f))
    wsg = np.ascontiguousarray(np.concatenate(
        [np.asarray(inputs["L0_attn_w_step"], f),
         np.asarray(inputs["L0_attn_w_gate"], f)], axis=1))
    wmd = np.ascontiguousarray(np.concatenate(
        [np.asarray(inputs["L0_attn_w_mom"], f),
         np.asarray(inputs["L0_attn_w_decay"], f)], axis=1))
    smg = np.ascontiguousarray(
        np.asarray(inputs["L0_attn_store_norm_g"], f).reshape(KC, 128).T)
    rmg = np.ascontiguousarray(
        np.asarray(inputs["L0_attn_retrieve_norm_g"], f).reshape(KC, 128).T)
    mw0 = np.ascontiguousarray(np.asarray(inputs["L0_attn_mem_w0"], f))
    mw1 = np.ascontiguousarray(np.asarray(inputs["L0_attn_mem_w1"], f))
    mhg = np.ascontiguousarray(
        np.asarray(inputs["L0_attn_mh_gamma"], f)[:, 0, :].T)
    rw = np.ascontiguousarray(np.asarray(inputs["L0_ffn_router_W"], f))
    rb = np.ascontiguousarray(
        np.asarray(inputs["L0_ffn_router_b"], f).reshape(1, NE))
    w1e = np.ascontiguousarray(np.asarray(inputs["L0_ffn_W1"], f))
    w2e = np.ascontiguousarray(np.asarray(inputs["L0_ffn_W2"], f))
    b1 = np.asarray(inputs["L0_ffn_b1"], f)
    b2 = np.asarray(inputs["L0_ffn_b2"], f)
    b1t = np.ascontiguousarray(
        b1.reshape(NE, 32, 128).transpose(2, 0, 1).reshape(128, NE * 32))
    b2t = np.ascontiguousarray(
        b2.reshape(NE, 8, 128).transpose(2, 0, 1).reshape(128, NE * 8))
    b1r = np.ascontiguousarray(b1.reshape(NE, DFF))
    b2r = np.ascontiguousarray(b2.reshape(NE, D))
    tri = np.triu(np.ones((128, 128), f), 1)
    tri8 = np.triu(np.ones((8, 8), f), 1)
    tokid = np.ascontiguousarray(
        (np.arange(32)[None, :] * 128 +
         np.arange(128)[:, None]).astype(np.int32))
    dcol = (np.arange(8)[None, :] * 128 +
            np.arange(128)[:, None]).astype(f)
    slot = np.broadcast_to(np.arange(24, dtype=f)[None, :] * 256.0,
                           (8, 24)).copy()

    ones = np.ones((128, 128), f)
    ident = np.eye(128, dtype=f)
    oh8 = np.zeros((8, 8 * 128), f)
    for h in range(8):
        oh8[h, h * 128:(h + 1) * 128] = 1.0
    ohc8 = np.zeros((128, 64), f)
    for h in range(8):
        ohc8[:, h * 8 + h] = 1.0 / 128.0
    bd8 = np.zeros((8, 64), f)
    for k in range(8):
        for c in range(8):
            bd8[k, c * 8 + k] = 1.0
    shared = dict(wkv=wkv, wq=wq, wcomb=wcomb, wsg=wsg, wmd=wmd, smg=smg,
                  rmg=rmg, mw0=mw0, mw1=mw1, mhg=mhg, rw=rw, rb=rb,
                  w1e=w1e, w2e=w2e, b1r=b1r, b2r=b2r,
                  c_tri=tri, c_tri8=tri8, c_tokid=tokid, c_dcol=dcol,
                  c_slot=slot,
                  c_ones=ones, c_id=ident, c_oh8=oh8, c_ohc8=ohc8,
                  c_bd8=bd8)
    in_maps = []
    for c in range(8):
        s, hf = divmod(c, 2)
        seq_fm = np.ascontiguousarray(seq[s].T)
        xwin = np.ascontiguousarray(seq[s, hf * 4096:(hf + 1) * 4096, :].T)
        gidx = np.zeros((128, KC), np.int32)
        for kc in range(KC):
            gidx[:, kc] = 2 * (kc * 128 + np.arange(128)) + hf
        m = dict(shared)
        m.update(seq_fm=seq_fm, xwin_fm=xwin, gidx=gidx)
        in_maps.append(m)
    return in_maps


def kernel(**inputs):
    in_maps = prep_inputs(inputs)
    nc = build_nc()
    res = run_bass_kernel_spmd(nc, in_maps, core_ids=list(range(8)))
    out = np.empty((4, N, D), np.float32)
    for c in range(8):
        s, hf = divmod(c, 2)
        out[s, hf * 4096:(hf + 1) * 4096, :] = \
            np.asarray(res.results[c]["out"])
    return out


# version 17
# speedup vs baseline: 1.2379x; 1.2379x over previous
import numpy as np
from contextlib import ExitStack

import concourse.bass as bass
import concourse.mybir as mybir
from concourse.bass_utils import run_bass_kernel_spmd
from concourse.bass_core_rust import IndirectOffsetOnAxis

F32 = mybir.dt.float32
BF16 = mybir.dt.bfloat16
I32 = mybir.dt.int32
AF = mybir.ActivationFunctionType
ALU = mybir.AluOpType
AX = mybir.AxisListType

D = 1024
N = 8192
KC = 8
H = 8
CH = 64
NT = 16
TT = 512
NST = 4
ST = 1024
DFF = 4096
NE = 8
CS = float(np.float32(0.01 * 2.0 / 128.0))
MAGIC = 0x5F3759DF

ATTN_ON = True


class Chan:
    def __init__(self, es, nc, name):
        self.sem = es.enter_context(nc.semaphore(name))
        self.n = 0

    def inc(self, inst, k=1):
        inst.then_inc(self.sem, k)
        self.n += k
        return self.n

    def wait(self, eng, val=None):
        eng.wait_ge(self.sem, self.n if val is None else val)


def _rsqrt(v, ch, out, s_ap, t1, t2, iters=2):
    def e(inst):
        ch.inc(inst)
        ch.wait(v)
    oi = out.bitcast(I32)
    si = s_ap.bitcast(I32)
    e(v.tensor_scalar(oi, si, 1, None, ALU.arith_shift_right))
    e(v.tensor_scalar(oi, oi, -1, MAGIC, ALU.mult, ALU.add))
    e(v.tensor_scalar(t2, s_ap, 0.5, None, ALU.mult))
    for _ in range(iters):
        e(v.tensor_mul(t1, out, out))
        e(v.tensor_mul(t1, t1, t2))
        e(v.tensor_scalar(t1, t1, -1.0, 1.5, ALU.mult, ALU.add))
        e(v.tensor_mul(out, out, t1))


def build_nc(attn=ATTN_ON, debug=False, p2stage=99, rcut=99, p1cut=99,
             ntile=NT, fcut=5, scut=99):
    nc = bass.Bass("TRN2", target_bir_lowering=False)

    d_seq = nc.dram_tensor("seq_fm", [D, N], F32, kind="ExternalInput").ap()
    d_xwin = nc.dram_tensor("xwin_fm", [D, 4096], F32, kind="ExternalInput").ap()
    d_gidx = nc.dram_tensor("gidx", [128, KC], I32, kind="ExternalInput").ap()
    d_wkv = nc.dram_tensor("wkv", [D, 2 * D], F32, kind="ExternalInput").ap()
    d_wq = nc.dram_tensor("wq", [D, D], F32, kind="ExternalInput").ap()
    d_wcomb = nc.dram_tensor("wcomb", [D, D], F32, kind="ExternalInput").ap()
    d_wsg = nc.dram_tensor("wsg", [D, 16], F32, kind="ExternalInput").ap()
    d_wmd = nc.dram_tensor("wmd", [D, 16], F32, kind="ExternalInput").ap()
    d_smg = nc.dram_tensor("smg", [128, KC], F32, kind="ExternalInput").ap()
    d_rmg = nc.dram_tensor("rmg", [128, KC], F32, kind="ExternalInput").ap()
    d_mw0 = nc.dram_tensor("mw0", [128, 128], F32, kind="ExternalInput").ap()
    d_mw1 = nc.dram_tensor("mw1", [128, 128], F32, kind="ExternalInput").ap()
    d_mhg = nc.dram_tensor("mhg", [128, H], F32, kind="ExternalInput").ap()
    d_rw = nc.dram_tensor("rw", [D, NE], F32, kind="ExternalInput").ap()
    d_rb = nc.dram_tensor("rb", [1, NE], F32, kind="ExternalInput").ap()
    d_w1 = nc.dram_tensor("w1e", [NE, D, DFF], F32, kind="ExternalInput").ap()
    d_w2 = nc.dram_tensor("w2e", [NE, DFF, D], F32, kind="ExternalInput").ap()
    d_b1r = nc.dram_tensor("b1r", [NE, DFF], F32, kind="ExternalInput").ap()
    d_b2r = nc.dram_tensor("b2r", [NE, D], F32, kind="ExternalInput").ap()
    d_tri = nc.dram_tensor("c_tri", [128, 128], F32, kind="ExternalInput").ap()
    d_tri8 = nc.dram_tensor("c_tri8", [8, 8], F32, kind="ExternalInput").ap()
    d_tokid = nc.dram_tensor("c_tokid", [128, 32], I32, kind="ExternalInput").ap()
    d_dcol = nc.dram_tensor("c_dcol", [128, 8], F32, kind="ExternalInput").ap()
    d_slot = nc.dram_tensor("c_slot", [8, 24], F32, kind="ExternalInput").ap()
    d_ones = nc.dram_tensor("c_ones", [128, 128], F32, kind="ExternalInput").ap()
    d_id = nc.dram_tensor("c_id", [128, 128], F32, kind="ExternalInput").ap()
    d_oh8 = nc.dram_tensor("c_oh8", [8, 8 * 128], F32, kind="ExternalInput").ap()
    d_ohc8 = nc.dram_tensor("c_ohc8", [128, 64], F32, kind="ExternalInput").ap()
    d_bd8 = nc.dram_tensor("c_bd8", [8, 64], F32, kind="ExternalInput").ap()

    d_out = nc.dram_tensor("out", [4096, D], F32, kind="ExternalOutput").ap()

    d_w1b = nc.dram_tensor("scr_w1b", [NE * D, DFF], BF16, kind="Internal").ap()
    d_w2b = nc.dram_tensor("scr_w2b", [NE * DFF, D], BF16, kind="Internal").ap()
    d_xtm = nc.dram_tensor("scr_xtm", [4224, D], BF16, kind="Internal").ap()
    d_ytm = nc.dram_tensor("scr_ytm", [4224, D], BF16, kind="Internal").ap()
    d_sid = nc.dram_tensor("scr_sid", [6144, 1], I32, kind="Internal").ap()
    d_gat = nc.dram_tensor("scr_gat", [4224, 1], F32, kind="Internal").ap()
    d_xmt = nc.dram_tensor("scr_xmt", [4096, D], BF16, kind="Internal").ap()
    t_scr = nc.dram_tensor("scr_attn", [D, N], BF16, kind="Internal")
    d_scr = t_scr.ap()
    d_scr_rows = t_scr.ap().rearrange("d (g n) -> (d g) n", g=2)
    d_scr_gp = t_scr.ap().rearrange("(k p) (g n) -> (p g) k n",
                                    k=8, p=128, g=2)

    dbg = {}

    def mkdbg(name, shape, dt=F32):
        if debug:
            dbg[name] = nc.dram_tensor("dbg_" + name, shape, dt,
                                       kind="ExternalOutput").ap()

    mkdbg("rstd0", [128, TT]); mkdbg("ss0", [128, KC, TT])
    mkdbg("sr0", [128, KC, TT])
    mkdbg("k0", [128, H, TT]); mkdbg("v0", [128, H, TT])
    mkdbg("q0", [128, H, TT])
    mkdbg("sig0", [8, 2, TT]); mkdbg("sg0", [8, 16])
    mkdbg("a0", [128, TT]); mkdbg("sd0", [128, TT]); mkdbg("g00", [128, TT])
    mkdbg("dz0", [128, TT]); mkdbg("gm64", [128, 64])
    mkdbg("gtm0", [128, 4, 128]); mkdbg("ktm0", [128, 4, 128])
    mkdbg("wc00", [128, 128]); mkdbg("wc10", [128, 128])
    mkdbg("vst0", [128, H, TT]); mkdbg("comb0", [128, KC, TT])
    mkdbg("xattn0", [128, KC, ST]); mkdbg("xmid0", [128, KC, ST])
    mkdbg("xffn0", [128, KC, ST])
    mkdbg("cnts", [8, 32]); mkdbg("offs", [8, 32]); mkdbg("grpexp", [1, 24])
    mkdbg("sid", [6144, 1], I32); mkdbg("gat", [4096, 1])
    mkdbg("xtm", [4096, D], BF16); mkdbg("ytm", [4096, D], BF16)

    es = ExitStack()

    def sb(name, shape, dt=F32):
        return es.enter_context(nc.sbuf_tensor(name, shape, dt))

    with es:
        PB = [es.enter_context(nc.psum_tensor(f"pb{i}", [128, 512], F32))
              for i in range(8)]

        def PBbf(i):
            return PB[i][:].bitcast(BF16)

        ones_f = sb("ones_f", [128, 128])
        ones_bf = sb("ones_bf", [128, 128], BF16)
        id_f = sb("id_f", [128, 128])
        id_bf = sb("id_bf", [128, 128], BF16)
        oh8_f = sb("oh8_f", [8, 8 * 128], BF16)
        ohc8_bf = sb("ohc8_bf", [128, 64], BF16)
        bd8_f = sb("bd8_f", [8, 64])
        rw_bf = sb("rw_bf", [128, KC, NE], BF16)
        rb_bf = sb("rb_bf", [1, NE], BF16)
        gidx_sb = sb("gidx_sb", [128, KC], I32)

        c_in = Chan(es, nc, "c_in")
        c_ing = Chan(es, nc, "c_ing")
        c_xg = Chan(es, nc, "c_xg")
        c_dbg = Chan(es, nc, "c_dbg")
        c_pe = Chan(es, nc, "c_pe")
        c_v = Chan(es, nc, "c_v")
        c_a = Chan(es, nc, "c_a")
        c_x = Chan(es, nc, "c_x")
        c_w = Chan(es, nc, "c_w")
        c_w1 = Chan(es, nc, "c_w1")
        c_w2 = Chan(es, nc, "c_w2")
        c_g = Chan(es, nc, "c_g")
        c_st = Chan(es, nc, "c_st")
        c_out = Chan(es, nc, "c_out")
        c_wc = Chan(es, nc, "c_wc")
        c_st2 = Chan(es, nc, "c_st2")
        c_w2a = Chan(es, nc, "c_w2a")
        c_w2b = Chan(es, nc, "c_w2b")
        c_w2c = Chan(es, nc, "c_w2c")
        c_scl = [Chan(es, nc, f"c_scl{i}") for i in range(3)]
        c_sx = [Chan(es, nc, f"c_sx{i}") for i in range(3)]

        conv_state = {"i": 0}

        def conv_drip(gp2, n=1):
            for _ in range(n):
                k = conv_state["i"]
                if k >= 128:
                    return
                conv_state["i"] = k + 1
                ti, j = divmod(k, 8)
                if ti < NE:
                    npr = D // 8
                    c_wc.inc(gp2.dma_start(
                        d_w1b[ti * D + j * npr:ti * D + (j + 1) * npr, :],
                        d_w1[ti][j * npr:(j + 1) * npr, :]), 16)
                else:
                    e = ti - NE
                    c_wc.inc(gp2.dma_start(
                        d_w2b.rearrange("(a jb) f -> a jb f", jb=32)
                        [e * 128:(e + 1) * 128, 4 * j:4 * j + 4, :],
                        d_w2[e][j * 512:(j + 1) * 512, :]
                        .rearrange("(jb p) f -> p jb f", jb=4, p=128)), 16)

        def conv_chunk(gp2, ti, pieces=8):
            conv_drip(gp2, pieces)

        def dripg(blk):
            blk.gpsimd(lambda gp2: conv_drip(gp2, 1))

        with nc.Block(no_gpsimd_drain=True) as blk:
            def _p0sync(sy):
                for dst, srcd in [
                    (ones_f[:], d_ones), (id_f[:], d_id), (bd8_f[:], d_bd8),
                    (gidx_sb[:], d_gidx),
                ]:
                    c_in.inc(sy.dma_start(dst, srcd), 16)
                sy.wait_ge(c_in.sem, c_in.n)
            blk.sync(_p0sync)

            def _p0gp(gp):
                for dst, srcd in [
                    (rw_bf[:], d_rw.rearrange("(k p) c -> p k c", p=128)),
                    (oh8_f[:], d_oh8),
                    (rb_bf[:], d_rb), (ohc8_bf[:], d_ohc8),
                ]:
                    c_ing.inc(gp.dma_start(dst, srcd), 16)
                gp.wait_ge(c_ing.sem, c_ing.n)
            blk.gpsimd(_p0gp)

        with nc.Block(no_gpsimd_drain=True) as blk:
            def _p0b(v):
                v.tensor_copy(ones_bf[:], ones_f[:])
                v.tensor_copy(id_bf[:], id_f[:])
            blk.vector(_p0b)

            def _p0sc(sc):
                sc.preload_act_table(AF.Sigmoid)
            blk.scalar(_p0sc)

        if attn:
            p1 = ExitStack()
            with p1:
                def s1(name, shape, dt=F32):
                    return p1.enter_context(nc.sbuf_tensor(name, shape, dt))

                wkv_bf = s1("wkv_bf", [128, KC, 2 * D], BF16)
                wq_bf = s1("wq_bf", [128, KC, D], BF16)
                wcomb_bf = s1("wcomb_bf", [128, KC, D], BF16)
                wsg_bf = s1("wsg_bf", [128, KC, 16], BF16)
                wmd_bf = s1("wmd_bf", [128, KC, 16], BF16)
                smg_f = s1("smg_f", [128, KC])
                rmg_f = s1("rmg_f", [128, KC])
                mw0_f = s1("mw0_f", [128, 128])
                mw1_f = s1("mw1_f", [128, 128])
                mw0_bf = s1("mw0_bf", [128, 128], BF16)
                mw1_bf = s1("mw1_bf", [128, 128], BF16)
                w1t_bf = s1("w1t_bf", [128, 128], BF16)
                mhg1_f = s1("mhg1_f", [128, H])
                xt = [s1(f"xt{i}", [128, KC, TT], BF16) for i in range(2)]
                xga = {}
                rstd = [s1(f"rstd{i}", [128, TT]) for i in range(2)]
                ssb = s1("ssb", [128, KC, TT], BF16)
                srb = s1("srb", [128, KC, TT], BF16)
                vstage = ssb
                vsq = srb
                sqb_lo = ssb[:].rearrange("p h t -> p (h t)").bitcast(F32)
                sqb_hi = srb[:].rearrange("p h t -> p (h t)").bitcast(F32)

                def sqb(kc):
                    v_ = sqb_lo if kc < 4 else sqb_hi
                    return v_[:, (kc % 4) * TT:(kc % 4 + 1) * TT]
                kb = s1("kb", [128, H, TT], BF16)
                vf = kb
                vb = s1("vb", [128, H, TT], BF16)
                cstage = vb
                _vbf = vb[:].rearrange("p h t -> p (h t)").bitcast(F32)
                nt1 = _vbf[:, 0:TT]
                nt2 = _vbf[:, TT:2 * TT]
                qb = s1("qb", [128, H, TT], BF16)
                sig = s1("sig", [8, 2, TT], BF16)
                lrtm = s1("lrtm", [128, 4, 8])
                cmb = s1("cmb", [128, KC, 8], BF16)
                cmf = s1("cmf", [128, KC, 8])
                sgt = s1("sgt", [8, 16])
                rhs8 = s1("rhs8", [8, 2, 64])
                gm64 = s1("gm64", [128, 64]); gd64 = s1("gd64", [128, 64])
                sgz = s1("sgz", [128, H, TT], BF16)
                gtm = sgz[:].rearrange("p h (s q) -> p h s q", s=4)
                sdb = s1("sdb", [128, H, TT], BF16)
                dztm = sdb[:].rearrange("p h (s q) -> p h s q", s=4)
                ab = s1("ab", [128, H, TT], BF16)
                wcb = ab[:].rearrange("p h t -> p (h t)").rearrange(
                    "p (a b c d) -> p a b c d", a=2, b=2, c=H)
                g0b = s1("g0b", [128, H, TT], BF16)
                _g0f = g0b[:].rearrange("p h t -> p (h t)")
                sgx = _g0f[:, 0:512].rearrange("p (h c) -> p h c", h=H)
                sxb = _g0f[:, 512:1024].rearrange("p (h c) -> p h c", h=H)
                comb8 = _g0f[0:8, 1024:1536]
                dzb = s1("dzb", [128, H, TT], BF16)
                _dzf = dzb[:].rearrange("p h t -> p (h t)").bitcast(F32)
                n8a = _dzf[0:8, 0:TT]
                n8b = _dzf[0:8, TT:2 * TT]
                rstd8 = _dzf[0:8, 2 * TT:3 * TT]
                n8c = _dzf[0:8, 3 * TT:4 * TT]
                ktm = s1("ktm", [128, H, 4, 128], BF16)
                atm = s1("atm", [128, H, 4, 128], BF16)
                xsgA = ktm[:].rearrange("p h s q -> p (h s q)").bitcast(
                    F32).rearrange("p (k t) -> p k t", t=TT)
                xsgB = atm[:].rearrange("p h s q -> p (h s q)").bitcast(
                    F32).rearrange("p (k t) -> p k t", t=TT)
                m0 = s1("m0", [128, H, 128]); m1 = s1("m1", [128, H, 128])
                u0 = s1("u0", [128, H, 128]); u1 = s1("u1", [128, H, 128])
                mw0e = mw0_f[:].unsqueeze(1).to_broadcast([128, H, 128])
                mw1e = mw1_f[:].unsqueeze(1).to_broadcast([128, H, 128])
                zzb = cstage[:, :, TT - 64:TT]

                with nc.Block(no_gpsimd_drain=True) as blk:
                    def _w1s(sy):
                        for dst, srcd in [
                            (smg_f[:], d_smg), (rmg_f[:], d_rmg),
                            (mw0_f[:], d_mw0), (mw1_f[:], d_mw1),
                            (mhg1_f[:], d_mhg),
                        ]:
                            c_in.inc(sy.dma_start(dst, srcd), 16)
                        sy.wait_ge(c_in.sem, c_in.n)
                    blk.sync(_w1s)

                    def _w1g(gp):
                        for dst, srcd in [
                            (wkv_bf[:],
                             d_wkv.rearrange("(k p) c -> p k c", p=128)),
                            (wq_bf[:],
                             d_wq.rearrange("(k p) c -> p k c", p=128)),
                            (wcomb_bf[:],
                             d_wcomb.rearrange("(k p) c -> p k c", p=128)),
                            (wsg_bf[:],
                             d_wsg.rearrange("(k p) c -> p k c", p=128)),
                            (wmd_bf[:],
                             d_wmd.rearrange("(k p) c -> p k c", p=128)),
                            (mw0_bf[:], d_mw0), (mw1_bf[:], d_mw1),
                        ]:
                            c_ing.inc(gp.dma_start(dst, srcd), 16)
                        gp.wait_ge(c_ing.sem, c_ing.n)
                    blk.gpsimd(_w1g)
                with nc.Block(no_gpsimd_drain=True) as blk:
                    def _w1t(t):
                        t.transpose(PBbf(0)[:, 0:128], mw1_bf[:], id_bf[:])
                    blk.tensor(_w1t)

                    def _w1v(v):
                        v.tensor_scalar(mhg1_f[:], mhg1_f[:], 1.0, None,
                                        ALU.add)
                        v.tensor_scalar(wmd_bf[:], wmd_bf[:], 1.0 / 64.0,
                                        None, ALU.mult)
                        v.memset(m0[:], 0.0); v.memset(m1[:], 0.0)
                        v.memset(u0[:], 0.0); v.memset(u1[:], 0.0)
                        v.memset(cstage[:, :, :], 0.0)
                    blk.vector(_w1v)
                with nc.Block(no_gpsimd_drain=True) as blk:
                    def _w1c(v):
                        v.tensor_copy(w1t_bf[:], PBbf(0)[:, 0:128])
                    blk.vector(_w1c)

                    def _pl(gp2):
                        conv_drip(gp2, 1)
                    blk.gpsimd(_pl)

                    def _pls(sy):
                        c_x.inc(sy.dma_start(
                            xsgA[:],
                            d_seq.rearrange("(k p) n -> p k n", p=128)
                            [:, 0:4, 0:TT]), 16)
                        c_x.inc(sy.dma_start(
                            xsgB[:],
                            d_seq.rearrange("(k p) n -> p k n", p=128)
                            [:, 4:8, 0:TT]), 16)
                        xga[0] = c_x.n
                    blk.sync(_pls)

                    def _plz(sy):
                        c_st.inc(sy.dma_start(
                            d_scr.rearrange("(k p) n -> p k n", p=128)
                            [:, :, 0:63],
                            cstage[:, :, 0:63]), 16)
                        sy.wait_ge(c_st.sem, c_st.n)
                    blk.sync(_plz)

                def stats_blocks(xtile, rout, xmark):
                    with nc.Block(no_gpsimd_drain=True) as blk:
                        def _cst(sc, xtile=xtile, xmark=xmark):
                            sc.wait_ge(c_x.sem, xmark)
                            for kc in range(KC):
                                srcv = xsgA if kc < 4 else xsgB
                                c_a.inc(sc.copy(xtile[:, kc, :],
                                                srcv[:, kc % 4, :]))
                            for kc in range(KC):
                                srcv = xsgA if kc < 4 else xsgB
                                c_a.inc(sc.activation(
                                    sqb(kc), srcv[:, kc % 4, :], AF.Square))
                        blk.scalar(_cst)

                        def _mm(t):
                            t.wait_ge(c_a.sem, c_a.n)
                            for kc in range(KC):
                                t.matmul(PB[7][:], ones_f[:], sqb(kc),
                                         start=(kc == 0), stop=(kc == KC - 1))
                        blk.tensor(_mm)
                    with nc.Block(no_gpsimd_drain=True) as blk:
                        def _nw(v, rout=rout):
                            def e(inst):
                                c_v.inc(inst); c_v.wait(v)
                            e(v.tensor_scalar(nt1, PB[7][:], 1.0 / D, 1e-6,
                                              ALU.mult, ALU.add))
                            _rsqrt(v, c_v, rout[:], nt1, nt2, sqb(0))
                        blk.vector(_nw)

                stats_blocks(xt[0], rstd[0], xga[0])

                for ti in range(min(NT, ntile)):
                    cur, nxt = ti % 2, (ti + 1) % 2

                    if ti + 1 < NT:
                        with nc.Block(no_gpsimd_drain=True) as blk:
                            def _pf(gp2):
                                conv_drip(gp2, 1)
                            blk.gpsimd(_pf)

                            def _pfs(sy, ti=ti):
                                c_x.inc(sy.dma_start(
                                    xsgA[:],
                                    d_seq.rearrange("(k p) n -> p k n", p=128)
                                    [:, 0:4,
                                     (ti + 1) * TT:(ti + 2) * TT]), 16)
                                c_x.inc(sy.dma_start(
                                    xsgB[:],
                                    d_seq.rearrange("(k p) n -> p k n", p=128)
                                    [:, 4:8,
                                     (ti + 1) * TT:(ti + 2) * TT]), 16)
                                xga[ti + 1] = c_x.n
                            blk.sync(_pfs)
                        stats_blocks(xt[nxt], rstd[nxt], xga[ti + 1])

                    if ti == NT - 1:
                        with nc.Block(no_gpsimd_drain=True) as blk:
                            def _zz(v):
                                v.memset(zzb, 0.0)
                            blk.vector(_zz)
                    if p1cut < 2:
                        continue
                    with nc.Block(no_gpsimd_drain=True) as blk:
                        def _ssr(v, cur=cur, nxt=nxt, ti=ti):
                            for kc in range(KC):
                                v.scalar_tensor_tensor(
                                    ssb[:, kc, :], xt[cur][:, kc, :],
                                    smg_f[:, kc:kc + 1], rstd[cur][:],
                                    ALU.mult, ALU.mult)
                            for kc in range(KC):
                                v.scalar_tensor_tensor(
                                    srb[:, kc, 0:TT - 63],
                                    xt[cur][:, kc, 63:TT],
                                    rmg_f[:, kc:kc + 1], rstd[cur][:, 63:TT],
                                    ALU.mult, ALU.mult)
                            if ti + 1 < NT:
                                for kc in range(KC):
                                    v.scalar_tensor_tensor(
                                        srb[:, kc, TT - 63:TT],
                                        xt[nxt][:, kc, 0:63],
                                        rmg_f[:, kc:kc + 1],
                                        rstd[nxt][:, 0:63],
                                        ALU.mult, ALU.mult)
                            else:
                                for kc in range(KC):
                                    v.tensor_copy(srb[:, kc, TT - 63:TT],
                                                  zzb[:, kc, 0:63])
                        blk.vector(_ssr)
                        dripg(blk)

                    if p1cut < 3:
                        continue
                    with nc.Block(no_gpsimd_drain=True) as blk:
                        a_base = c_a.n

                        def _ckv(t, a_base=a_base):
                            for oc in range(16):
                                if oc >= 4:
                                    t.wait_ge(c_a.sem, a_base + oc - 3)
                                for kc in range(KC):
                                    ii = t.matmul(PB[oc % 4][:],
                                                  wkv_bf[:, kc,
                                                         oc * 128:
                                                         (oc + 1) * 128],
                                                  ssb[:, kc, :],
                                                  start=(kc == 0),
                                                  stop=(kc == KC - 1))
                                c_pe.inc(ii)
                            for oc in range(8):
                                if oc >= 2:
                                    t.wait_ge(c_a.sem, a_base + 15 + oc)
                                for kc in range(KC):
                                    ii = t.matmul(PB[4 + oc % 2][:],
                                                  wq_bf[:, kc,
                                                        oc * 128:
                                                        (oc + 1) * 128],
                                                  srb[:, kc, :],
                                                  start=(kc == 0),
                                                  stop=(kc == KC - 1))
                                c_pe.inc(ii)
                            for kc in range(KC):
                                ii = t.matmul(PB[6][0:8, :],
                                              wsg_bf[:, kc, 0:8],
                                              ssb[:, kc, :],
                                              start=(kc == 0),
                                              stop=(kc == KC - 1))
                            c_pe.inc(ii)
                            for kc in range(KC):
                                ii = t.matmul(PB[7][0:8, :],
                                              wsg_bf[:, kc, 8:16],
                                              srb[:, kc, :],
                                              start=(kc == 0),
                                              stop=(kc == KC - 1))
                            c_pe.inc(ii)
                        blk.tensor(_ckv)

                        pe_base = c_pe.n - 26

                        def _cev(sc, pe_base=pe_base):
                            for oc in range(16):
                                sc.wait_ge(c_pe.sem, pe_base + oc + 1)
                                dstt = kb if oc < 8 else vb
                                c_a.inc(sc.copy(dstt[:, oc % 8, :],
                                                PB[oc % 4][:]))
                            for oc in range(8):
                                sc.wait_ge(c_pe.sem, pe_base + 16 + oc + 1)
                                c_a.inc(sc.copy(qb[:, oc, :],
                                                PB[4 + oc % 2][:]))
                            sc.wait_ge(c_pe.sem, pe_base + 25)
                            c_a.inc(sc.activation(sig[:, 0, :], PB[6][0:8, :],
                                                  AF.Sigmoid))
                            sc.wait_ge(c_pe.sem, pe_base + 26)
                            c_a.inc(sc.activation(sig[:, 1, :], PB[7][0:8, :],
                                                  AF.Sigmoid))
                        blk.scalar(_cev)

                        def _ccm(v):
                            for kc in range(KC):
                                ii = v.tensor_reduce(
                                    cmf[:, kc, :],
                                    ssb[:, kc, :]
                                    .rearrange("p (c u) -> p c u", c=8),
                                    AX.X, ALU.add)
                            c_v.inc(ii); c_v.wait(v)
                            for kc in range(KC):
                                v.tensor_copy(cmb[:, kc, :], cmf[:, kc, :])
                        blk.vector(_ccm)
                        dripg(blk)

                    if p1cut < 4:
                        continue
                    with nc.Block(no_gpsimd_drain=True) as blk:
                        def _g1(t):
                            for kc in range(KC):
                                t.matmul(PB[0][0:8, 0:8], wmd_bf[:, kc, 0:8],
                                         cmb[:, kc, :],
                                         start=(kc == 0), stop=(kc == KC - 1))
                            for kc in range(KC):
                                ii = t.matmul(PB[0][0:8, 8:16],
                                              wmd_bf[:, kc, 8:16],
                                              cmb[:, kc, :],
                                              start=(kc == 0),
                                              stop=(kc == KC - 1))
                            c_pe.inc(ii)
                            for sub in range(4):
                                ii = t.transpose(
                                    PBbf(1)[:, sub * 8:(sub + 1) * 8],
                                    sig[0:8, 0, sub * 128:(sub + 1) * 128],
                                    id_bf[0:8, 0:8])
                            c_pe.inc(ii)
                        blk.tensor(_g1)

                        pe_g = c_pe.n

                        def _g2(sc, pe_g=pe_g):
                            sc.wait_ge(c_pe.sem, pe_g - 1)
                            c_a.inc(sc.activation(sgt[:, 0:8],
                                                  PB[0][0:8, 0:8], AF.Sigmoid))
                            c_a.inc(sc.activation(sgt[:, 8:16],
                                                  PB[0][0:8, 8:16],
                                                  AF.Sigmoid, scale=-1.0))
                        blk.scalar(_g2)

                        a_g = c_a.n

                        def _g3(v, pe_g=pe_g, a_g=a_g):
                            v.wait_ge(c_pe.sem, pe_g)
                            v.tensor_copy(
                                lrtm[:],
                                PBbf(1)[:, 0:32].rearrange("p (s c) -> p s c",
                                                           s=4))
                            v.wait_ge(c_a.sem, a_g)
                            bd3 = bd8_f[:].rearrange("h (c u) -> h c u",
                                                     u=8)
                            v.tensor_mul(rhs8[:, 0, :]
                                         .rearrange("h (c u) -> h c u", u=8),
                                         sgt[:, 0:8].unsqueeze(2)
                                         .to_broadcast([8, 8, 8]), bd3)
                            v.tensor_mul(rhs8[:, 1, :]
                                         .rearrange("h (c u) -> h c u", u=8),
                                         sgt[:, 8:16].unsqueeze(2)
                                         .to_broadcast([8, 8, 8]), bd3)
                        blk.vector(_g3)
                        dripg(blk)
                    with nc.Block(no_gpsimd_drain=True) as blk:
                        def _g4(t):
                            t.matmul(PB[2][:, 0:64], ones_f[0:8, :],
                                     rhs8[:, 0, :], start=True, stop=True)
                            c_pe.inc(t.matmul(PB[3][:, 0:64], ones_f[0:8, :],
                                              rhs8[:, 1, :], start=True,
                                              stop=True))
                        blk.tensor(_g4)

                        pe_g4 = c_pe.n

                        def _g5(v, pe_g4=pe_g4):
                            v.wait_ge(c_pe.sem, pe_g4)
                            v.tensor_copy(gm64[:], PB[2][:, 0:64])
                            v.tensor_copy(gd64[:], PB[3][:, 0:64])
                        blk.vector(_g5)
                        dripg(blk)

                    if p1cut < 5:
                        continue
                    for hg in range(2):
                        hs = list(range(hg * 4, hg * 4 + 4))
                        with nc.Block(no_gpsimd_drain=True) as blk:
                            def _d1(t, hs=hs):
                                for h in hs:
                                    c_pe.inc(t.matmul(PB[h % 4][:], mw0_bf[:],
                                                      kb[:, h, :], start=True,
                                                      stop=True))
                            blk.tensor(_d1)

                            pe_d = c_pe.n - 4

                            def _d2(sc, hs=hs, pe_d=pe_d):
                                for i, h in enumerate(hs):
                                    sc.wait_ge(c_pe.sem, pe_d + i + 1)
                                    c_a.inc(sc.activation(sgz[:, h, :],
                                                          PB[h % 4][:],
                                                          AF.Sigmoid))
                            blk.scalar(_d2)

                            a_d = c_a.n - 4

                            def _d3(v, hs=hs, a_d=a_d):
                                for i, h in enumerate(hs):
                                    v.wait_ge(c_a.sem, a_d + i + 1)
                                    ii = v.tensor_mul(ab[:, h, :],
                                                      PB[h % 4][:],
                                                      sgz[:, h, :])
                                c_v.inc(ii); c_v.wait(v)
                                for h in hs:
                                    ii = v.tensor_mul(sdb[:, h, :],
                                                      ab[:, h, :],
                                                      sgz[:, h, :])
                                c_v.inc(ii); c_v.wait(v)
                                for h in hs:
                                    ii = v.tensor_sub(sdb[:, h, :],
                                                      ab[:, h, :],
                                                      sdb[:, h, :])
                                c_v.inc(ii); c_v.wait(v)
                                for h in hs:
                                    v.tensor_add(sdb[:, h, :], sgz[:, h, :],
                                                 sdb[:, h, :])
                            blk.vector(_d3)
                            dripg(blk)
                        with nc.Block(no_gpsimd_drain=True) as blk:
                            def _d4(t, hs=hs):
                                for h in hs:
                                    c_pe.inc(t.matmul(PB[4 + h % 4][:],
                                                      mw1_bf[:],
                                                      ab[:, h, :], start=True,
                                                      stop=True))
                            blk.tensor(_d4)

                            pe_d4 = c_pe.n - 4

                            def _d5(v, hs=hs, pe_d4=pe_d4):
                                for i, h in enumerate(hs):
                                    v.wait_ge(c_pe.sem, pe_d4 + i + 1)
                                    v.tensor_sub(g0b[:, h, :],
                                                 PB[4 + h % 4][:], vb[:, h, :])
                            blk.vector(_d5)
                            dripg(blk)
                        with nc.Block(no_gpsimd_drain=True) as blk:
                            def _d6(t, hs=hs):
                                for h in hs:
                                    c_pe.inc(t.matmul(PB[h % 4][:], w1t_bf[:],
                                                      g0b[:, h, :],
                                                      start=True, stop=True))
                            blk.tensor(_d6)

                            pe_d6 = c_pe.n - 4

                            def _d7(v, hs=hs, pe_d6=pe_d6):
                                for i, h in enumerate(hs):
                                    v.wait_ge(c_pe.sem, pe_d6 + i + 1)
                                    v.tensor_mul(dzb[:, h, :], PB[h % 4][:],
                                                 sdb[:, h, :])
                            blk.vector(_d7)
                            dripg(blk)

                    if p1cut < 6:
                        continue
                    if debug and ti == 0:
                        with nc.Block(no_gpsimd_drain=True) as blk:
                            def _dbgE(gp9):
                                for nm, srcd in [("a0", ab[:, 0, :]),
                                                 ("sd0", sdb[:, 0, :]),
                                                 ("g00", g0b[:, 0, :]),
                                                 ("dz0", dzb[:, 0, :])]:
                                    if nm in dbg:
                                        c_dbg.inc(gp9.dma_start(dbg[nm],
                                                                srcd), 16)
                                gp9.wait_ge(c_dbg.sem, c_dbg.n)
                            blk.gpsimd(_dbgE)

                    if p1cut < 6:
                        continue
                    with nc.Block(no_gpsimd_drain=True) as blk:
                        srcs = []
                        for src, dst, scaled in ((kb, ktm, False),
                                                 (ab, atm, False),
                                                 (g0b, gtm, True),
                                                 (dzb, dztm, True)):
                            for h in range(H):
                                srcs.append((src, dst, scaled, h))
                        a_base = c_a.n
                        v_base = c_v.n

                        def _e1(t, srcs=srcs, a_base=a_base, v_base=v_base):
                            for gi, (src, dst, scaled, h) in enumerate(srcs):
                                bank = gi % 4
                                if 4 <= gi < 20:
                                    t.wait_ge(c_a.sem, a_base + gi - 3)
                                elif gi >= 20:
                                    t.wait_ge(c_v.sem, v_base + gi - 19)
                                for sq_ in range(4):
                                    ii = t.transpose(
                                        PBbf(bank)[:, sq_ * 256:
                                                   sq_ * 256 + 128],
                                        src[:, h,
                                            sq_ * 128:(sq_ + 1) * 128],
                                        id_bf[:])
                                c_pe.inc(ii)
                        blk.tensor(_e1)

                        pe_e = c_pe.n - len(srcs)

                        def _e2s(sc, srcs=srcs, pe_e=pe_e):
                            for gi, (src, dst, scaled, h) in enumerate(srcs):
                                if scaled:
                                    continue
                                bank = gi % 4
                                sc.wait_ge(c_pe.sem, pe_e + gi + 1)
                                pv = PBbf(bank).rearrange(
                                    "p (s q) -> p s q", s=4)[:, :, 0:128]
                                c_a.inc(sc.copy(dst[:, h, :, :], pv))
                        blk.scalar(_e2s)

                        def _e2(v, srcs=srcs, pe_e=pe_e):
                            for gi, (src, dst, scaled, h) in enumerate(srcs):
                                if not scaled:
                                    continue
                                bank = gi % 4
                                v.wait_ge(c_pe.sem, pe_e + gi + 1)
                                pv = PBbf(bank).rearrange(
                                    "p (s q) -> p s q", s=4)[:, :, 0:128]
                                ii = v.scalar_tensor_tensor(
                                    dst[:, h, :, :], pv, CS,
                                    lrtm[:, :, h:h + 1]
                                    .to_broadcast([128, 4, 128]),
                                    ALU.mult, ALU.mult)
                                c_v.inc(ii)
                        blk.vector(_e2)
                        dripg(blk)

                    if p1cut < 7:
                        continue
                    with nc.Block(no_gpsimd_drain=True) as blk:
                        pe_dwa = [0] * 8; pe_dwb = [0] * 8
                        pe_x0 = [0] * 8; pe_vl = [0] * 8
                        v_sx = [0] * 8; v_msub = [0] * 8; v_w = [0] * 8
                        a_sgx = [0] * 8; a_vev = [0] * 8
                        cc = c_pe.n
                        for cl in range(8):
                            cc += 1; pe_dwa[cl] = cc
                            cc += 1; pe_dwb[cl] = cc
                            if cl >= 1:
                                cc += 1; pe_x0[cl - 1] = cc
                                cc += 1; pe_vl[cl - 1] = cc
                        cc += 1; pe_x0[7] = cc
                        cc += 1; pe_vl[7] = cc
                        vv = c_v.n
                        for cl in range(8):
                            vv += 1
                            if cl >= 1:
                                vv += 1; v_sx[cl - 1] = vv
                            vv += 1; v_msub[cl] = vv
                            vv += 1
                            vv += 1
                            vv += 1; v_w[cl] = vv
                        vv += 1; v_sx[7] = vv
                        aa = c_a.n
                        for cl in range(8):
                            aa += 1; a_sgx[cl] = aa
                            if cl >= 1:
                                aa += 1; a_vev[cl - 1] = aa
                        aa += 1; a_vev[7] = aa
                        sgf = _g0f[:, 0:512]
                        sxf = _g0f[:, 512:1024]
                        _xtf = xt[cur][:].rearrange(
                            "p k t -> p (k t)").bitcast(F32)
                        tmp0 = _xtf[:, 0:1024].rearrange(
                            "p (h c) -> p h c", h=H)
                        tmp1 = _xtf[:, 1024:2048].rearrange(
                            "p (h c) -> p h c", h=H)

                        def pb4(i):
                            return PB[i][:].rearrange("p (h c) -> p h c", h=4)

                        def _f_pe(t):
                            def dw(cl):
                                sq_ = cl // 2
                                rb_ = (cl % 2) * 64
                                if cl >= 1:
                                    t.wait_ge(c_v.sem, v_msub[cl - 1])
                                for hh in range(2):
                                    for h in range(hh * 4, hh * 4 + 4):
                                        co = (h % 4) * 128
                                        t.matmul(
                                            PB[hh][:, co:co + 128],
                                            ktm[rb_:rb_ + 64, h, sq_, :],
                                            dztm[rb_:rb_ + 64, h, sq_, :],
                                            start=True, stop=True)
                                        ii = t.matmul(
                                            PB[2 + hh][:, co:co + 128],
                                            atm[rb_:rb_ + 64, h, sq_, :],
                                            gtm[rb_:rb_ + 64, h, sq_, :],
                                            start=True, stop=True)
                                    c_pe.inc(ii)
                                assert c_pe.n == pe_dwb[cl]

                            def retr(cl):
                                par = cl % 2
                                xb_ = 4 if par == 0 else 6
                                vb2 = 5 if par == 0 else 7
                                t.wait_ge(c_v.sem, v_w[cl])
                                if cl >= 2:
                                    t.wait_ge(c_a.sem, a_sgx[cl - 2])
                                for h in range(H):
                                    ii = t.matmul(
                                        PB[xb_][:, h * 64:h * 64 + 64],
                                        wcb[:, par, 0, h, :],
                                        qb[:, h, cl * 64:cl * 64 + 64],
                                        start=True, stop=True)
                                c_pe.inc(ii)
                                assert c_pe.n == pe_x0[cl]
                                t.wait_ge(c_v.sem, v_sx[cl])
                                if cl >= 2:
                                    t.wait_ge(c_a.sem, a_vev[cl - 2])
                                for h in range(H):
                                    ii = t.matmul(
                                        PB[vb2][:, h * 64:h * 64 + 64],
                                        wcb[:, par, 1, h, :],
                                        sxb[:, h, :],
                                        start=True, stop=True)
                                c_pe.inc(ii)
                                assert c_pe.n == pe_vl[cl]

                            for cl in range(8):
                                dw(cl)
                                if cl >= 1:
                                    retr(cl - 1)
                            retr(7)
                        blk.tensor(_f_pe)

                        def _f_v(v):
                            for cl in range(8):
                                par = cl % 2
                                c8 = cl * 8
                                gmbc = gm64[:, c8:c8 + 8].unsqueeze(2) \
                                    .to_broadcast([128, H, 128])
                                gdbc = gd64[:, c8:c8 + 8].unsqueeze(2) \
                                    .to_broadcast([128, H, 128])
                                v.tensor_mul(m0[:], m0[:], gmbc)
                                c_v.inc(v.tensor_mul(m1[:], m1[:], gmbc))
                                if cl >= 1:
                                    pxb = 4 if (cl - 1) % 2 == 0 else 6
                                    v.wait_ge(c_a.sem, a_sgx[cl - 1])
                                    if cl >= 2:
                                        v.wait_ge(c_pe.sem, pe_vl[cl - 2])
                                    c_v.inc(v.tensor_mul(
                                        sxf, PB[pxb][:, 0:512], sgf))
                                    assert c_v.n == v_sx[cl - 1]
                                v.wait_ge(c_pe.sem, pe_dwa[cl])
                                c_v.wait(v)
                                v.tensor_sub(m0[:, 0:4, :], m0[:, 0:4, :],
                                             pb4(0))
                                v.tensor_sub(m1[:, 0:4, :], m1[:, 0:4, :],
                                             pb4(2))
                                v.wait_ge(c_pe.sem, pe_dwb[cl])
                                v.tensor_sub(m0[:, 4:8, :], m0[:, 4:8, :],
                                             pb4(1))
                                ii = v.tensor_sub(m1[:, 4:8, :],
                                                  m1[:, 4:8, :], pb4(3))
                                c_v.inc(ii)
                                assert c_v.n == v_msub[cl]
                                c_v.wait(v)
                                v.tensor_mul(tmp0, u0[:], gdbc)
                                c_v.inc(v.tensor_mul(tmp1, u1[:], gdbc))
                                c_v.wait(v)
                                v.tensor_add(u0[:], tmp0, m0[:])
                                c_v.inc(v.tensor_add(u1[:], tmp1, m1[:]))
                                c_v.wait(v)
                                if cl >= 2:
                                    v.wait_ge(c_pe.sem, pe_vl[cl - 2])
                                v.tensor_add(wcb[:, par, 0, :, :], u0[:],
                                             mw0e)
                                ii = v.tensor_add(wcb[:, par, 1, :, :],
                                                  u1[:], mw1e)
                                c_v.inc(ii)
                                assert c_v.n == v_w[cl]
                            v.wait_ge(c_a.sem, a_sgx[7])
                            v.wait_ge(c_pe.sem, pe_vl[6])
                            c_v.inc(v.tensor_mul(sxf, PB[6][:, 0:512], sgf))
                            assert c_v.n == v_sx[7]
                        blk.vector(_f_v)

                        def _f_a(sc):
                            for cl in range(8):
                                par = cl % 2
                                xb_ = 4 if par == 0 else 6
                                if cl >= 1:
                                    sc.wait_ge(c_v.sem, v_sx[cl - 1])
                                sc.wait_ge(c_pe.sem, pe_x0[cl])
                                c_a.inc(sc.activation(
                                    sgf, PB[xb_][:, 0:512], AF.Sigmoid))
                                assert c_a.n == a_sgx[cl]
                                if cl >= 1:
                                    pvb = 5 if (cl - 1) % 2 == 0 else 7
                                    sc.wait_ge(c_pe.sem, pe_vl[cl - 1])
                                    c_a.inc(sc.copy(
                                        vstage[:, :,
                                               (cl - 1) * 64:cl * 64],
                                        PB[pvb][:].rearrange(
                                            "p (h c) -> p h c", h=H)))
                                    assert c_a.n == a_vev[cl - 1]
                            sc.wait_ge(c_pe.sem, pe_vl[7])
                            c_a.inc(sc.copy(
                                vstage[:, :, 7 * 64:8 * 64],
                                PB[7][:].rearrange("p (h c) -> p h c", h=H)))
                            assert c_a.n == a_vev[7]
                        blk.scalar(_f_a)
                        dripg(blk)

                    if p1cut < 8:
                        continue
                    with nc.Block(no_gpsimd_drain=True) as blk:
                        def _gv1(sc):
                            for h in range(H):
                                ii = sc.activation(vsq[:, h, :],
                                                   vstage[:, h, :],
                                                   AF.Square)
                            c_a.inc(ii)
                        blk.scalar(_gv1)

                        def _gp1(t):
                            t.wait_ge(c_a.sem, c_a.n)
                            for h in range(H):
                                t.matmul(PB[0][0:8, :],
                                         ohc8_bf[:, h * 8:(h + 1) * 8],
                                         vsq[:, h, :],
                                         start=(h == 0), stop=(h == H - 1))
                        blk.tensor(_gp1)
                        dripg(blk)
                    with nc.Block(no_gpsimd_drain=True) as blk:
                        def _gv2(v):
                            def e(inst):
                                c_v.inc(inst); c_v.wait(v)
                            e(v.tensor_scalar(n8a, PB[0][0:8, :], 1.0,
                                              1e-6, ALU.mult, ALU.add))
                            _rsqrt(v, c_v, rstd8, n8a, n8b, n8c)
                            e(v.tensor_mul(comb8, rstd8, sig[:, 1, :]))
                        blk.vector(_gv2)
                        dripg(blk)
                    with nc.Block(no_gpsimd_drain=True) as blk:
                        v_g = c_v.n

                        def _gp2(t, v_g=v_g):
                            for h in range(H):
                                if h >= 2:
                                    t.wait_ge(c_v.sem, v_g + h - 1)
                                c_pe.inc(t.matmul(
                                    PB[1 + h % 2][:],
                                    oh8_f[:, h * 128:(h + 1) * 128],
                                    comb8, start=True, stop=True))
                        blk.tensor(_gp2)

                        pe_gb = c_pe.n - 8

                        def _gv3(v, pe_gb=pe_gb):
                            for h in range(H):
                                v.wait_ge(c_pe.sem, pe_gb + h + 1)
                                c_v.inc(v.scalar_tensor_tensor(
                                    vf[:, h, :], vstage[:, h, :],
                                    mhg1_f[:, h:h + 1], PB[1 + h % 2][:],
                                    ALU.mult, ALU.mult))
                        blk.vector(_gv3)
                        dripg(blk)
                    with nc.Block(no_gpsimd_drain=True) as blk:
                        a_cb = c_a.n

                        def _gp3(t, a_cb=a_cb):
                            for oc in range(KC):
                                if oc >= 2:
                                    t.wait_ge(c_a.sem, a_cb + oc - 1)
                                for h in range(H):
                                    ii = t.matmul(PB[2 + oc % 2][:],
                                                  wcomb_bf[:, h,
                                                           oc * 128:
                                                           (oc + 1) * 128],
                                                  vf[:, h, :],
                                                  start=(h == 0),
                                                  stop=(h == H - 1))
                                c_pe.inc(ii)
                        blk.tensor(_gp3)

                        pe_cb = c_pe.n - 8

                        def _ga1(sc, pe_cb=pe_cb):
                            for oc in range(KC):
                                sc.wait_ge(c_pe.sem, pe_cb + oc + 1)
                                c_a.inc(sc.copy(cstage[:, oc, :],
                                                PB[2 + oc % 2][:]))
                        blk.scalar(_ga1)

                        def _gs1(sy, ti=ti):
                            sy.wait_ge(c_a.sem, c_a.n)
                            wdt = TT if ti < NT - 1 else TT - 63
                            c_st.inc(sy.dma_start(
                                d_scr.rearrange("(k p) n -> p k n", p=128)
                                [:, :, 63 + ti * TT:63 + ti * TT + wdt],
                                cstage[:, :, 0:wdt]), 16)
                            sy.wait_ge(c_st.sem, c_st.n)
                        blk.sync(_gs1)
                        dripg(blk)

                    if debug and ti == 0:
                        with nc.Block(no_gpsimd_drain=True) as blk:
                            def _dbg(gp9):
                                for nm, src in [
                                    ("rstd0", rstd[0][:]), ("ss0", ssb[:]),
                                    ("sr0", srb[:]), ("k0", kb[:]),
                                    ("v0", vb[:]), ("q0", qb[:]),
                                    ("sig0", sig[:]), ("sg0", sgt[:]),
                                    ("gm64", gm64[:]),
                                    ("gtm0", gtm[:, 0, :, :]),
                                    ("ktm0", ktm[:, 0, :, :]),
                                    ("wc00", wcb[:, 1, 0, 0, :]),
                                    ("wc10", wcb[:, 1, 1, 0, :]),
                                    ("vst0", vstage[:]),
                                    ("comb0", cstage[:]),
                                ]:
                                    if nm in dbg:
                                        c_dbg.inc(gp9.dma_start(dbg[nm], src),
                                                  16)
                                gp9.wait_ge(c_dbg.sem, c_dbg.n)
                            blk.gpsimd(_dbg)
        else:
            pz = ExitStack()
            with pz:
                zt = pz.enter_context(
                    nc.sbuf_tensor("zt_stub", [128, KC, TT], BF16))
                with nc.Block(no_gpsimd_drain=True) as blk:
                    def _z1(v):
                        v.memset(zt[:], 0.0)
                    blk.vector(_z1)
                with nc.Block(no_gpsimd_drain=True) as blk:
                    def _z2(sy):
                        for ti in range(NT):
                            c_st.inc(sy.dma_start(
                                d_scr.rearrange("(k p) n -> p k n", p=128)
                                [:, :, ti * TT:(ti + 1) * TT], zt[:]), 16)
                        sy.wait_ge(c_st.sem, c_st.n)
                    blk.sync(_z2)

        if not attn:
            with nc.Block(no_gpsimd_drain=True) as blk:
                def _cva(gp):
                    for ti in range(16):
                        conv_chunk(gp, ti)
                blk.gpsimd(_cva)

        NGR = 24
        p2 = ExitStack()
        with p2:
            def s2(name, shape, dt=F32):
                return p2.enter_context(nc.sbuf_tensor(name, shape, dt))

            w1g = s2("w1g", [128, KC, DFF], BF16)
            xffn = w1g[:, :, 0:ST]
            xm_bf = w1g[:, :, ST:2 * ST]
            xm_bf2 = w1g[:, :, 2 * ST:3 * ST]
            hbuf = s2("hbuf", [128, 8192], BF16)
            xattn = hbuf.rearrange("p (a n) -> p a n", a=KC)
            ybf = xattn
            hv = hbuf.rearrange("p (f s) -> p f s", s=256)
            w2rot = s2("w2rot", [128, 3, 8, 1024], BF16)
            xattnB = w2rot[:, 0, :, :]
            ybfB = xattnB
            xmB = None
            xw512 = s2("xw512", [128, KC, 512])
            sq2 = s2("sq2", [128, KC, 512], BF16)
            muf = s2("muf", [128, 512]); s2f = s2("s2f", [128, 512])
            varf = s2("varf", [128, 512]); rstdf = s2("rstdf", [128, 512])
            exf = s2("exf", [128, 64]); mxn = s2("mxn", [128, 8])
            eqf = s2("eqf", [128, 64]); omf = s2("omf", [128, 64])
            prf = s2("prf", [128, 64]); fstf = s2("fstf", [128, 64])
            ssumf = s2("ssumf", [128, 8]); gatef = s2("gatef", [128, 8])
            zrow = s2("zrow", [128, 64])
            fstb = s2("fstb", [128, 32, 8], BF16)
            cnts = s2("cnts", [8, 32]); incl = s2("incl", [8, 32])
            offs = s2("offs", [8, 32]); dgall = s2("dgall", [8, 256])
            toti = s2("toti", [8, 1], I32); padi = s2("padi", [8, 1], I32)
            padf = s2("padf", [8, 1]); basec = s2("basec", [8, 1])
            gm24 = s2("gm24", [8, 24]); grpexp = s2("grpexp", [1, 24])
            dmat = s2("dmat", [128, 8]); destf = s2("destf", [128, 1])
            desti2 = s2("desti2", [128, 2], I32)
            oobt = s2("oobt", [128, 48], I32)
            tokid_sb = s2("tokid_sb", [128, 32], I32)
            dcol_sb = s2("dcol_sb", [128, 8])
            fcol_sb = s2("fcol_sb", [128, 32])
            slot_sb = s2("slot_sb", [8, 24])
            tri_bf = s2("tri_bf", [128, 128], BF16)
            tri8_sb = s2("tri8_sb", [8, 8])
            sidx3 = s2("sidx3", [128, 6], I32)
            gsl2 = s2("gsl2", [128, 4])
            egc = s2("egc", [128, 1]); eg8 = s2("eg8", [128, 1])
            ecs = s2("ecs", [128, 1])
            b8f = s2("b8f", [8, 1]); bi8 = s2("bi8", [8, 1], I32)
            widxf = s2("widxf", [128, 2, KC])
            widx = s2("widx", [128, 2, KC], I32)
            widx2f = s2("widx2f", [128, 2, 32])
            widx2 = s2("widx2", [128, 2, 32], I32)
            b1g8 = s2("b1g8", [8, 2, 512]); b2g8 = s2("b2g8", [8, 2, 128])
            b1fm = s2("b1fm", [128, 32]); b2fm = s2("b2fm", [128, 8])
            xg2 = s2("xg2", [128, 4, 1024], BF16)
            xg_fm = s2("xg_fm", [128, KC, 256], BF16)
            yg_fm = s2("yg_fm", [128, KC, 256], BF16)
            yg_tm = s2("yg_tm", [128, 2, 1024], BF16)

            with nc.Block(no_gpsimd_drain=True) as blk:
                def _pt(sc):
                    sc.preload_act_table(AF.Exp)
                blk.scalar(_pt)

                def _ptv(v):
                    v.memset(zrow[:], 0.0)
                    v.memset(offs[:], 0.0)
                    v.memset(oobt[:], 4096)
                    v.memset(xg2[:], 0.0)
                blk.vector(_ptv)

                def _pts(sy):
                    for dst, srcd in [(tokid_sb[:], d_tokid),
                                      (dcol_sb[:], d_dcol),
                                      (slot_sb[:], d_slot),
                                      (tri8_sb[:], d_tri8)]:
                        c_x.inc(sy.dma_start(dst, srcd), 16)
                    sy.wait_ge(c_x.sem, c_x.n)
                blk.sync(_pts)

                def _ptg(gp):
                    c_g.inc(gp.dma_start(tri_bf[:], d_tri), 16)
                    gp.wait_ge(c_g.sem, c_g.n)
                blk.gpsimd(_ptg)
            with nc.Block(no_gpsimd_drain=True) as blk:
                def _pre2g(gp):
                    c_g.inc(gp.dma_start(
                        d_sid.rearrange("(p a) c -> p (a c)", p=128),
                        oobt[:]), 16)
                    c_g.inc(gp.dma_start(d_xtm[4096:4224, :],
                                         xg2[:, 0, :]), 16)
                    c_g.inc(gp.dma_start(d_gat[4096:4224, :],
                                         zrow[:, 0:1]), 16)
                    gp.wait_ge(c_g.sem, c_g.n)
                blk.gpsimd(_pre2g)

                def _fc(v):
                    ii = None
                    for a in range(4):
                        ii = v.tensor_scalar(fcol_sb[:, a * 8:(a + 1) * 8],
                                             dcol_sb[:], 1.0,
                                             float(1024 * a),
                                             ALU.mult, ALU.add)
                    c_v.inc(ii)
                blk.vector(_fc)

            g_xa = {}
            if p2stage >= 1:
                with nc.Block(no_gpsimd_drain=True) as blk:
                    def _ga0(gp):
                        for kc in range(KC):
                            c_g.inc(gp.indirect_dma_start(
                                xattn[:, kc, :], None,
                                d_scr_rows,
                                IndirectOffsetOnAxis(
                                    gidx_sb[:, kc:kc + 1], 0),
                                element_offset=0), 16)
                        g_xa[0] = (c_g, c_g.n)
                    blk.gpsimd(_ga0)
            for st_i in range(NST):
                scol = st_i * ST
                xat = xattn if st_i % 2 == 0 else xattnB

                if p2stage < 1:
                    continue
                if st_i + 1 < NST:
                    with nc.Block(no_gpsimd_drain=True) as blk:
                        def _ga(gp, st_i=st_i):
                            dst = xattnB if (st_i + 1) % 2 == 1 else xattn
                            ch = (c_g, c_w2)[(st_i + 1) % 2]
                            for kc in range(KC):
                                ch.inc(gp.indirect_dma_start(
                                    dst[:, kc, :], None,
                                    d_scr_rows,
                                    IndirectOffsetOnAxis(
                                        gidx_sb[:, kc:kc + 1], 0),
                                    element_offset=(st_i + 1) * ST), 16)
                            g_xa[st_i + 1] = (ch, ch.n)
                        blk.gpsimd(_ga)

                def ln_pass(xin, xout, with_xwin, scol, gwait=None):
                    for bj in range(ST // 512):
                        b0 = bj * 512
                        with nc.Block(no_gpsimd_drain=True) as blk:
                            if with_xwin:
                                def _lw(sy, scol=scol, b0=b0):
                                    c_x.inc(sy.dma_start(
                                        xw512[:],
                                        d_xwin.rearrange(
                                            "(k p) n -> p k n", p=128)
                                        [:, :, scol + b0:scol + b0 + 512]),
                                        16)
                                    sy.wait_ge(c_x.sem, c_x.n)
                                blk.sync(_lw)

                            def _l1(sc, xin=xin, b0=b0, gwait=gwait,
                                    bj=bj):
                                if gwait is not None and bj == 0:
                                    sc.wait_ge(gwait[0].sem, gwait[1])
                                for kc in range(KC):
                                    ii = sc.activation(
                                        sq2[:, kc, :],
                                        xin[:, kc, b0:b0 + 512], AF.Square)
                                c_a.inc(ii)
                            blk.scalar(_l1)

                            def _l2(t, xin=xin, b0=b0):
                                t.wait_ge(c_a.sem, c_a.n)
                                for kc in range(KC):
                                    t.matmul(PB[0][:], ones_bf[:],
                                             xin[:, kc, b0:b0 + 512],
                                             start=(kc == 0),
                                             stop=(kc == KC - 1))
                                for kc in range(KC):
                                    t.matmul(PB[1][:], ones_bf[:],
                                             sq2[:, kc, :],
                                             start=(kc == 0),
                                             stop=(kc == KC - 1))
                            blk.tensor(_l2)
                        with nc.Block(no_gpsimd_drain=True) as blk:
                            def _l3(v):
                                def e(inst):
                                    c_v.inc(inst); c_v.wait(v)
                                e(v.tensor_scalar(muf[:], PB[0][:], 1.0 / D,
                                                  None, ALU.mult))
                                e(v.tensor_scalar(s2f[:], PB[1][:], 1.0 / D,
                                                  None, ALU.mult))
                                e(v.tensor_mul(varf[:], muf[:], muf[:]))
                                e(v.scalar_tensor_tensor(varf[:], varf[:],
                                                         -1.0, s2f[:],
                                                         ALU.mult, ALU.add))
                                e(v.tensor_scalar(varf[:], varf[:], 1.0,
                                                  1e-5, ALU.mult, ALU.add))
                                _rsqrt(v, c_v, rstdf[:], varf[:], s2f[:],
                                       varf[:])
                            blk.vector(_l3)
                        with nc.Block(no_gpsimd_drain=True) as blk:
                            def _l4(v, xin=xin, xout=xout, b0=b0,
                                    with_xwin=with_xwin):
                                for kc in range(KC):
                                    ii = v.tensor_sub(sq2[:, kc, :],
                                                      xin[:, kc, b0:b0 + 512],
                                                      muf[:])
                                c_v.inc(ii); c_v.wait(v)
                                for kc in range(KC):
                                    if with_xwin:
                                        ii = v.tensor_mul(sq2[:, kc, :],
                                                          sq2[:, kc, :],
                                                          rstdf[:])
                                    else:
                                        ii = v.tensor_mul(
                                            xout[:, kc, b0:b0 + 512],
                                            sq2[:, kc, :], rstdf[:])
                                c_v.inc(ii); c_v.wait(v)
                                if with_xwin:
                                    for kc in range(KC):
                                        v.tensor_add(xout[:, kc, b0:b0 + 512],
                                                     sq2[:, kc, :],
                                                     xw512[:, kc, :])
                            blk.vector(_l4)

                if p2stage < 2:
                    continue
                ln_pass(xat, xm_bf, True, scol, gwait=g_xa[st_i])
                ln_pass(xm_bf, xffn, False, scol)

                if p2stage < 3:
                    continue
                with nc.Block(no_gpsimd_drain=True) as blk:
                    def _r1(t):
                        for sub in range(8):
                            for kc in range(KC):
                                t.matmul(PB[0][:, sub * 8:sub * 8 + 8],
                                         xffn[:, kc,
                                              sub * 128:(sub + 1) * 128],
                                         rw_bf[:, kc, :],
                                         start=(kc == 0), stop=False)
                            ii = t.matmul(PB[0][:, sub * 8:sub * 8 + 8],
                                          ones_bf[0:1, :], rb_bf[:],
                                          start=False, stop=True)
                        c_pe.inc(ii)
                    blk.tensor(_r1)

                    if rcut < 1:
                        continue

                    def _r2(v):
                        v.wait_ge(c_pe.sem, c_pe.n)
                        for sub in range(8):
                            ii = v.reduce_max(mxn[:, sub:sub + 1],
                                              PB[0][:, sub * 8:sub * 8 + 8],
                                              AX.X, negate=True)
                        c_v.inc(ii); c_v.wait(v)
                        for sub in range(8):
                            ii = v.tensor_scalar(
                                eqf[:, sub * 8:sub * 8 + 8],
                                PB[0][:, sub * 8:sub * 8 + 8],
                                mxn[:, sub:sub + 1], 0.0,
                                ALU.add, ALU.is_ge)
                        c_v.inc(ii)
                    blk.vector(_r2)

                    if rcut < 2:
                        continue

                    def _r4(sc):
                        sc.wait_ge(c_v.sem, c_v.n)
                        for sub in range(8):
                            c_a.inc(sc.activation(
                                exf[:, sub * 8:sub * 8 + 8],
                                PB[0][:, sub * 8:sub * 8 + 8], AF.Exp,
                                bias=mxn[:, sub:sub + 1]))
                    blk.scalar(_r4)
                if rcut < 3:
                    continue
                with nc.Block(no_gpsimd_drain=True) as blk:
                    def _r5(v, st_i=st_i):
                        for sub in range(8):
                            ii = v.tensor_scalar(omf[:, sub * 8:sub * 8 + 8],
                                                 eqf[:, sub * 8:sub * 8 + 8],
                                                 -1.0, 1.0, ALU.mult,
                                                 ALU.add)
                        c_v.inc(ii)
                        c_v.wait(v)
                        for sub in range(8):
                            v.reduce_sum(ssumf[:, sub:sub + 1],
                                         exf[:, sub * 8:sub * 8 + 8], AX.X)
                        for sub in range(8):
                            ii = v.tensor_tensor_scan(
                                prf[:, sub * 8:sub * 8 + 8],
                                omf[:, sub * 8:sub * 8 + 8],
                                zrow[:, sub * 8:sub * 8 + 8],
                                1.0, ALU.mult, ALU.add)
                        c_v.inc(ii)
                        c_v.wait(v)
                        v.reciprocal(gatef[:], ssumf[:])
                        for sub in range(8):
                            v.tensor_copy(fstf[:, sub * 8:sub * 8 + 1],
                                          eqf[:, sub * 8:sub * 8 + 1])
                        for sub in range(8):
                            ii = v.tensor_mul(fstf[:, sub * 8 + 1:sub * 8 + 8],
                                              eqf[:, sub * 8 + 1:sub * 8 + 8],
                                              prf[:, sub * 8:sub * 8 + 7])
                        c_v.inc(ii)
                        c_v.wait(v)
                        for sub in range(8):
                            ii = v.tensor_copy(fstb[:, st_i * 8 + sub, :],
                                               fstf[:, sub * 8:sub * 8 + 8])
                        c_v.inc(ii)
                    blk.vector(_r5)

                    def _r6(sy, st_i=st_i):
                        sy.wait_ge(c_v.sem, c_v.n)
                        for sub in range(8):
                            c_st.inc(sy.dma_start(
                                d_gat.rearrange("(a p) c -> p a c", p=128)
                                [:, st_i * 8 + sub, :],
                                gatef[:, sub:sub + 1]), 16)
                        sy.wait_ge(c_st.sem, c_st.n)
                    blk.sync(_r6)

                    def _r7(t, st_i=st_i):
                        t.wait_ge(c_v.sem, c_v.n)
                        ii = None
                        for sub in range(8):
                            ii = t.matmul(PB[3][0:8, sub:sub + 1],
                                          fstb[:, st_i * 8 + sub, :],
                                          ones_bf[:, 0:1],
                                          start=True, stop=True)
                        c_pe.inc(ii)
                    blk.tensor(_r7)
                with nc.Block(no_gpsimd_drain=True) as blk:
                    def _r8(v, st_i=st_i):
                        c_v.inc(v.tensor_copy(cnts[:, st_i * 8:st_i * 8 + 8],
                                              PB[3][0:8, 0:8]))
                    blk.vector(_r8)

                if p2stage < 4:
                    continue
                peT = c_pe.n
                vbT = c_v.n
                stT = [c_st.n, c_st2.n]
                with nc.Block(no_gpsimd_drain=True) as blk:
                    def _t1(t, peT=peT, vbT=vbT):
                        for tc in range(16):
                            if tc >= 2:
                                t.wait_ge(c_v.sem, vbT + tc - 1)
                            bk = 2 + 2 * (tc % 2)
                            src = xffn if tc < 8 else xm_bf
                            co = (tc % 8) * 128
                            ii = None
                            for kc in range(KC):
                                ii = t.transpose(
                                    PBbf(bk)[:, kc * 128:(kc + 1) * 128],
                                    src[:, kc, co:co + 128],
                                    id_bf[:])
                            c_pe.inc(ii)
                    blk.tensor(_t1)

                    def _t2(v, peT=peT, stT=stT):
                        for tc in range(16):
                            v.wait_ge(c_pe.sem, peT + tc + 1)
                            if tc >= 2:
                                ch = (c_st, c_st2)[tc % 2]
                                v.wait_ge(ch.sem,
                                          stT[tc % 2] + 16 * (tc // 2))
                            bk = 2 + 2 * (tc % 2)
                            c_v.inc(v.tensor_copy(xg2[:, tc % 2, :],
                                                  PBbf(bk)[:, 0:1024]))
                    blk.vector(_t2)

                    def _t3(sy, st_i=st_i, vbT=vbT):
                        for tc in range(16):
                            sy.wait_ge(c_v.sem, vbT + tc + 1)
                            ch = (c_st, c_st2)[tc % 2]
                            if tc < 8:
                                dst = d_xtm.rearrange(
                                    "(a p) d -> p a d", p=128)[
                                    :, st_i * 8 + tc, :]
                            else:
                                dst = d_xmt.rearrange(
                                    "(a p) d -> p a d", p=128)[
                                    :, st_i * 8 + (tc - 8), :]
                            ch.inc(sy.dma_start(dst, xg2[:, tc % 2, :]), 16)
                        sy.wait_ge(c_st.sem, c_st.n)
                        sy.wait_ge(c_st2.sem, c_st2.n)
                    blk.sync(_t3)

            if scut >= 2 and p2stage >= 4:
                with nc.Block(no_gpsimd_drain=True) as blk:
                    def _s1(v):
                        def e(inst):
                            c_v.inc(inst); c_v.wait(v)
                        e(v.tensor_tensor_scan(incl[:], cnts[:],
                                               zrow[0:8, 0:32], 0.0,
                                               ALU.add, ALU.add))
                        e(v.tensor_copy(toti[:], incl[:, 31:32]))
                        e(v.tensor_scalar(padi[:], toti[:], 255, None,
                                          ALU.bitwise_and))
                        e(v.tensor_scalar(padi[:], padi[:], -1, 256,
                                          ALU.mult, ALU.add))
                        e(v.tensor_scalar(padi[:], padi[:], 255, None,
                                          ALU.bitwise_and))
                        e(v.tensor_tensor(padi[:], toti[:], padi[:],
                                          ALU.add))
                        e(v.tensor_copy(padf[:], padi[:]))
                    blk.vector(_s1)
                with nc.Block(no_gpsimd_drain=True) as blk:
                    def _s2t(t):
                        c_pe.inc(t.matmul(PB[0][0:8, 0:1], tri8_sb[:],
                                          padf[:], start=True, stop=True))
                    blk.tensor(_s2t)

                    def _s2v(v):
                        v.wait_ge(c_pe.sem, c_pe.n)
                        def e(inst):
                            c_v.inc(inst); c_v.wait(v)
                        e(v.tensor_copy(basec[:], PB[0][0:8, 0:1]))
                        e(v.tensor_copy(offs[:, 1:32], incl[:, 0:31]))
                        e(v.tensor_scalar(offs[:], offs[:], basec[:, 0:1],
                                          None, ALU.add))
                        e(v.tensor_scalar(gm24[:], slot_sb[:],
                                          basec[:, 0:1], 0.0,
                                          ALU.subtract, ALU.is_ge))
                        ii = None
                        for c in range(32):
                            ii = v.tensor_scalar(dgall[:, c * 8:(c + 1) * 8],
                                                 id_f[0:8, 0:8],
                                                 offs[:, c:c + 1],
                                                 None, ALU.mult)
                        c_v.inc(ii); c_v.wait(v)
                    blk.vector(_s2v)
                with nc.Block(no_gpsimd_drain=True) as blk:
                    def _s3t(t):
                        c_pe.inc(t.matmul(PB[1][0:1, 0:24], ones_f[0:8, 0:1],
                                          gm24[:], start=True, stop=True))
                    blk.tensor(_s3t)

                    def _s3v(v):
                        v.wait_ge(c_pe.sem, c_pe.n)
                        c_v.inc(v.tensor_scalar(grpexp[:], PB[1][0:1, 0:24],
                                                1.0, -1.0, ALU.mult,
                                                ALU.add))
                        c_v.wait(v)
                    blk.vector(_s3v)
                pebS = c_pe.n
                vbS = c_v.n
                gbS = [c_g.n, c_w2.n]
                with nc.Block(no_gpsimd_drain=True) as blk:
                    def _s4t(t, pebS=pebS, vbS=vbS):
                        for c in range(32):
                            if c >= 2:
                                t.wait_ge(c_v.sem, vbS + 5 * (c - 2) + 5)
                            c_pe.inc(t.matmul(PB[c % 2][:, 0:8], tri_bf[:],
                                              fstb[:, c, :],
                                              start=True, stop=True))
                            c_pe.inc(t.matmul(PB[2 + c % 2][:, 0:8],
                                              ones_f[0:8, :],
                                              dgall[0:8, c * 8:(c + 1) * 8],
                                              start=True, stop=True))
                    blk.tensor(_s4t)

                    def _s4v(v, pebS=pebS, gbS=gbS):
                        for c in range(32):
                            v.wait_ge(c_pe.sem, pebS + 2 * c + 2)
                            if c >= 2:
                                ch = (c_g, c_w2)[c % 2]
                                v.wait_ge(ch.sem,
                                          gbS[c % 2] + 16 * (c // 2))
                            def e(inst):
                                c_v.inc(inst); c_v.wait(v)
                            e(v.tensor_copy(dmat[:], PB[c % 2][:, 0:8]))
                            e(v.tensor_tensor(dmat[:], dmat[:],
                                              PB[2 + c % 2][:, 0:8],
                                              ALU.add))
                            e(v.tensor_tensor(dmat[:], dmat[:],
                                              fstb[:, c, :], ALU.mult))
                            e(v.reduce_sum(destf[:], dmat[:], AX.X))
                            e(v.tensor_copy(desti2[:, c % 2:c % 2 + 1],
                                            destf[:]))
                    blk.vector(_s4v)

                    def _s4g(gp, vbS=vbS):
                        for c in range(32):
                            gp.wait_ge(c_v.sem, vbS + 5 * c + 5)
                            ch = (c_g, c_w2)[c % 2]
                            ch.inc(gp.indirect_dma_start(
                                d_sid, IndirectOffsetOnAxis(
                                    desti2[:, c % 2:c % 2 + 1], 0),
                                tokid_sb[:, c:c + 1], None,
                                oob_is_err=False), 16)
                        gp.wait_ge(c_g.sem, c_g.n)
                        gp.wait_ge(c_w2.sem, c_w2.n)
                    blk.gpsimd(_s4g)

            if scut >= 3 and p2stage >= 4:
                def _mark(d, k, ch):
                    if k in d:
                        assert d[k] == ch.n, (k, d[k], ch.n)
                    else:
                        d[k] = ch.n

                g_sm = {}; g_scat = {}
                w1A = {}; w1B = {}; w2m = {}
                x_sidx = {}
                v_widx = {}

                def smch(g):
                    return (c_g, c_w2)[g % 2]

                def w2ch(S):
                    return (c_w2a, c_w2b, c_w2c)[S % 3]

                def smalls(gp, g):
                    sl3 = (g % 3) * 2
                    sl2 = (g % 2) * 2
                    ch = smch(g)
                    for a in range(2):
                        ch.inc(gp.indirect_dma_start(
                            xg2[:, sl2 + a, :], None, d_xtm,
                            IndirectOffsetOnAxis(
                                sidx3[:, sl3 + a:sl3 + a + 1], 0),
                            oob_is_err=False), 16)
                        ch.inc(gp.indirect_dma_start(
                            gsl2[:, sl2 + a:sl2 + a + 1], None, d_gat,
                            IndirectOffsetOnAxis(
                                sidx3[:, sl3 + a:sl3 + a + 1], 0),
                            oob_is_err=False), 16)
                    ch.inc(gp.indirect_dma_start(
                        b1g8[0:8, g % 2, :], None,
                        d_b1r.rearrange("e (a q) -> (e a) q", a=8),
                        IndirectOffsetOnAxis(bi8[:, 0:1], 0),
                        oob_is_err=False), 16)
                    ch.inc(gp.indirect_dma_start(
                        b2g8[0:8, g % 2, :], None,
                        d_b2r.rearrange("e (a q) -> (e a) q", a=8),
                        IndirectOffsetOnAxis(bi8[:, 0:1], 0),
                        oob_is_err=False), 16)
                    _mark(g_sm, g, ch)

                d_w2v = d_w2b.rearrange("(a j) f -> a (j f)", j=32)

                def w1gather(gp, g, half):
                    off = half * 2048
                    ch = (c_w, c_w1)[half]
                    for kc in range(KC):
                        ch.inc(gp.indirect_dma_start(
                            w1g[:, kc, off:off + 2048], None, d_w1b,
                            IndirectOffsetOnAxis(
                                widx[:, g % 2, kc:kc + 1], 0),
                            element_offset=off), 16)
                    _mark(w1A if half == 0 else w1B, g, ch)

                def w2gather(gp, S):
                    gg, s = divmod(S, 4)
                    sl = S % 3
                    ch = w2ch(S)
                    ch.inc(gp.indirect_dma_start(
                        w2rot[:, sl, :, :].rearrange(
                            "p j f -> p (j f)"), None,
                        d_w2v,
                        IndirectOffsetOnAxis(
                            widx2[:, gg % 2, 0:1], 0),
                        element_offset=s * 8 * D), 16)
                    _mark(w2m, S, ch)

                def scat_y(gp, g):
                    sl3p = (g % 3) * 2
                    ch = c_scl[g % 3]
                    for a in range(2):
                        ch.inc(gp.indirect_dma_start(
                            d_ytm, IndirectOffsetOnAxis(
                                sidx3[:, sl3p + a:sl3p + a + 1], 0),
                            yg_tm[:, a, :], None,
                            oob_is_err=False), 16)
                        if a == 0:
                            gp.wait_ge(ch.sem, ch.n)
                    _mark(g_scat, g, ch)

                def widx_chain(v, g):
                    sl = g % 2
                    def e(inst):
                        c_v.inc(inst); c_v.wait(v)
                    e(v.tensor_copy(egc[:], PB[2][:, 0:1]))
                    e(v.tensor_scalar(eg8[:], egc[:], 8.0, None, ALU.mult))
                    e(v.tensor_scalar(b8f[:], dcol_sb[0:8, 0:1],
                                      eg8[0:8, 0:1], None, ALU.add))
                    e(v.tensor_copy(bi8[:], b8f[:]))
                    e(v.tensor_scalar(ecs[:], egc[:], 1024.0, None,
                                      ALU.mult))
                    e(v.tensor_scalar(widxf[:, sl, :], dcol_sb[:],
                                      ecs[:, 0:1], None, ALU.add))
                    e(v.tensor_copy(widx[:, sl, :], widxf[:, sl, :]))
                    e(v.tensor_scalar(ecs[:], egc[:], 128.0, None,
                                      ALU.mult))
                    e(v.tensor_scalar(widx2f[:, sl, 0:1], dcol_sb[:, 0:1],
                                      ecs[:, 0:1], None, ALU.add))
                    e(v.tensor_copy(widx2[:, sl, 0:1],
                                    widx2f[:, sl, 0:1]))
                    _mark(v_widx, g, c_v)

                with nc.Block(no_gpsimd_drain=True) as blk:
                    def _p3t(t):
                        c_pe.inc(t.matmul(PB[2][:, 0:1], ones_f[0:1, :],
                                          grpexp[0:1, 0:1], start=True,
                                          stop=True))
                    blk.tensor(_p3t)

                    def _p3s(sy):
                        for a in range(2):
                            c_sx[0].inc(sy.dma_start(
                                sidx3[:, a:a + 1],
                                d_sid.rearrange("(b p) c -> p b c", p=128)
                                [:, a, :]), 16)
                        _mark(x_sidx, 0, c_sx[0])
                        sy.wait_ge(c_sx[0].sem, c_sx[0].n)
                    blk.sync(_p3s)

                    def _p3v(v):
                        v.wait_ge(c_pe.sem, c_pe.n)
                        widx_chain(v, 0)
                    blk.vector(_p3v)
                with nc.Block(no_gpsimd_drain=True) as blk:
                    def _p3g(gp):
                        gp.wait_ge(c_wc.sem, c_wc.n)
                        smalls(gp, 0)
                        w1gather(gp, 0, 0)
                        w1gather(gp, 0, 1)
                        w2gather(gp, 0)
                        w2gather(gp, 1)
                        w2gather(gp, 2)
                    blk.gpsimd(_p3g)

                pe_inT = {}; pe_eg = {}; pe_bT = {}
                pe_p1 = {}; pe_p2 = {}; pe_outT = {}
                cc = c_pe.n
                for g in range(NGR):
                    cc += 1; pe_inT[g] = cc
                    if g + 1 < NGR:
                        cc += 1; pe_eg[g + 1] = cc
                    cc += 1; pe_bT[g] = cc
                    for fi in range(32):
                        cc += 1; pe_p1[(g, fi)] = cc
                    for j in range(32):
                        cc += 1; pe_p2[(g, j)] = cc
                    cc += 1; pe_outT[g] = cc
                pe_end = cc

                v_xfm = {}; v_bias = {}; v_yfm = {}; v_ytm = {}
                cc = c_v.n
                for g in range(NGR):
                    cc += 2; v_xfm[g] = cc
                    if g + 1 < NGR:
                        cc += 10; v_widx[g + 1] = cc
                    cc += 2; v_bias[g] = cc
                    cc += 8; v_yfm[g] = cc
                    cc += 2; v_ytm[g] = cc
                v_end = cc

                a_b3 = c_a.n

                def a_h3(g, fi):
                    return a_b3 + 32 * g + fi + 1
                a_end = a_b3 + 32 * NGR

                xbase = [c.n for c in c_sx]
                per3 = [0, 0, 0]
                for g in range(1, NGR):
                    per3[g % 3] += 32
                    x_sidx[g] = xbase[g % 3] + per3[g % 3]

                with nc.Block(no_gpsimd_drain=True) as blk:
                    def _m_g(gp):
                        for g in range(NGR):
                            if g >= 1:
                                gp.wait_ge(c_v.sem, v_ytm[g - 1])
                                scat_y(gp, g - 1)
                            if g + 1 < NGR:
                                gp.wait_ge(c_sx[(g + 1) % 3].sem,
                                           x_sidx[g + 1])
                                gp.wait_ge(c_v.sem, v_widx[g + 1])
                                gp.wait_ge(c_pe.sem, pe_bT[g])
                                smalls(gp, g + 1)
                                gp.wait_ge(c_pe.sem, pe_p1[(g, 15)])
                                w1gather(gp, g + 1, 0)
                                gp.wait_ge(c_pe.sem, pe_p1[(g, 31)])
                                w1gather(gp, g + 1, 1)
                            for S in range(4 * g + 3, 4 * g + 7):
                                if S >= 4 * NGR:
                                    continue
                                gs2, ss2 = divmod(S - 3, 4)
                                gp.wait_ge(c_pe.sem,
                                           pe_p2[(gs2, ss2 * 8 + 7)])
                                w2gather(gp, S)
                        gp.wait_ge(c_v.sem, v_ytm[NGR - 1])
                        scat_y(gp, NGR - 1)
                        for ch in c_scl:
                            gp.wait_ge(ch.sem, ch.n)
                        gp.wait_ge(c_g.sem, c_g.n)
                        gp.wait_ge(c_w2.sem, c_w2.n)
                        gp.wait_ge(c_w.sem, c_w.n)
                        gp.wait_ge(c_w1.sem, c_w1.n)
                        gp.wait_ge(c_w2a.sem, c_w2a.n)
                        gp.wait_ge(c_w2b.sem, c_w2b.n)
                        gp.wait_ge(c_w2c.sem, c_w2c.n)
                    blk.gpsimd(_m_g)

                    def _m_s(sy):
                        for g in range(1, NGR):
                            if g >= 3:
                                sy.wait_ge(c_scl[g % 3].sem, g_scat[g - 3])
                            sl3 = (g % 3) * 2
                            for a in range(2):
                                c_sx[g % 3].inc(sy.dma_start(
                                    sidx3[:, sl3 + a:sl3 + a + 1],
                                    d_sid.rearrange("(b p) c -> p b c",
                                                    p=128)
                                    [:, 2 * g + a, :]), 16)
                            _mark(x_sidx, g, c_sx[g % 3])
                        for cs in c_sx:
                            sy.wait_ge(cs.sem, cs.n)
                    blk.sync(_m_s)

                    def _m_pe(t):
                        for g in range(NGR):
                            if g >= 1:
                                t.wait_ge(c_v.sem, v_ytm[g - 1])
                            else:
                                t.wait_ge(c_v.sem, v_widx[0])
                            t.wait_ge(smch(g).sem, g_sm[g])
                            sl2 = (g % 2) * 2
                            ii = None
                            for a in range(2):
                                for kc in range(KC):
                                    ii = t.transpose(
                                        PBbf(6 + a)[:, kc * 128:
                                                    (kc + 1) * 128],
                                        xg2[:, sl2 + a,
                                            kc * 128:(kc + 1) * 128],
                                        id_bf[:])
                            c_pe.inc(ii)
                            assert c_pe.n == pe_inT[g]
                            if g + 1 < NGR:
                                c_pe.inc(t.matmul(
                                    PB[2][:, 0:1], ones_f[0:1, :],
                                    grpexp[0:1, g + 1:g + 2],
                                    start=True, stop=True))
                                assert c_pe.n == pe_eg[g + 1]
                            for a4 in range(4):
                                t.transpose(PB[3][:, a4 * 8:(a4 + 1) * 8],
                                            b1g8[0:8, g % 2,
                                                 a4 * 128:(a4 + 1) * 128],
                                            id_f[0:8, 0:8])
                            ii = t.transpose(PB[3][:, 32:40],
                                             b2g8[0:8, g % 2, :],
                                             id_f[0:8, 0:8])
                            c_pe.inc(ii)
                            assert c_pe.n == pe_bT[g]
                            for fi in range(32):
                                if fi == 0:
                                    t.wait_ge(c_w.sem, w1A[g])
                                    t.wait_ge(c_v.sem, v_xfm[g])
                                if fi == 16:
                                    t.wait_ge(c_w1.sem, w1B[g])
                                if fi >= 2:
                                    t.wait_ge(c_a.sem, a_h3(g, fi - 2))
                                elif g >= 1:
                                    t.wait_ge(c_a.sem, a_h3(g - 1, 30 + fi))
                                for kc in range(KC):
                                    ii = t.matmul(
                                        PB[fi % 2][:, 0:256],
                                        w1g[:, kc,
                                            fi * 128:(fi + 1) * 128],
                                        xg_fm[:, kc, :],
                                        start=(kc == 0),
                                        stop=(kc == KC - 1))
                                c_pe.inc(ii)
                                assert c_pe.n == pe_p1[(g, fi)]
                            t.wait_ge(c_v.sem, v_bias[g])
                            t.wait_ge(c_a.sem, a_h3(g, 31))
                            for j in range(32):
                                s = j // 8
                                if j % 8 == 0:
                                    t.wait_ge(w2ch(4 * g + s).sem,
                                              w2m[4 * g + s])
                                for o in range(8):
                                    if o == 0:
                                        t.wait_ge(c_a.sem, a_h3(g, j))
                                    ii = t.matmul(
                                        PB[o][:, 0:256],
                                        w2rot[:, (4 * g + s) % 3, j % 8,
                                              o * 128:(o + 1) * 128],
                                        hv[:, j, :],
                                        start=(j == 0), stop=(j == 31))
                                c_pe.inc(ii)
                                assert c_pe.n == pe_p2[(g, j)]
                            t.wait_ge(c_v.sem, v_yfm[g])
                            for a in range(2):
                                for oc in range(8):
                                    ii = t.transpose(
                                        PBbf(7 - a)[:, oc * 128:
                                                    (oc + 1) * 128],
                                        yg_fm[:, oc, a * 128:(a + 1) * 128],
                                        id_bf[:])
                            c_pe.inc(ii)
                            assert c_pe.n == pe_outT[g]
                        assert c_pe.n == pe_end
                    blk.tensor(_m_pe)

                    def _m_v(v):
                        for g in range(NGR):
                            v.wait_ge(c_pe.sem, pe_inT[g])
                            c_v.inc(v.tensor_copy(
                                xg_fm[:, :, 0:128],
                                PBbf(6).rearrange("p (k s) -> p k s",
                                                  s=128)))
                            c_v.inc(v.tensor_copy(
                                xg_fm[:, :, 128:256],
                                PBbf(7).rearrange("p (k s) -> p k s",
                                                  s=128)))
                            assert c_v.n == v_xfm[g]
                            if g + 1 < NGR:
                                v.wait_ge(c_pe.sem, pe_eg[g + 1])
                                v.wait_ge(smch(g).sem, g_sm[g])
                                widx_chain(v, g + 1)
                                assert c_v.n == v_widx[g + 1]
                            v.wait_ge(c_pe.sem, pe_bT[g])
                            c_v.inc(v.tensor_copy(b1fm[:], PB[3][:, 0:32]))
                            c_v.inc(v.tensor_copy(b2fm[:], PB[3][:, 32:40]))
                            assert c_v.n == v_bias[g]
                            v.wait_ge(c_pe.sem, pe_p2[(g, 31)])
                            v.wait_ge(c_v.sem, v_bias[g])
                            for oc in range(8):
                                c_v.inc(v.tensor_scalar(
                                    yg_fm[:, oc, :],
                                    PB[oc][:, 0:256],
                                    b2fm[:, oc:oc + 1], None, ALU.add))
                            assert c_v.n == v_yfm[g]
                            v.wait_ge(c_pe.sem, pe_outT[g])
                            if g >= 1:
                                v.wait_ge(c_scl[(g - 1) % 3].sem,
                                          g_scat[g - 1])
                            sl2 = (g % 2) * 2
                            for a in range(2):
                                c_v.inc(v.tensor_scalar(
                                    yg_tm[:, a, :],
                                    PBbf(7 - a)[:, 0:1024],
                                    gsl2[:, sl2 + a:sl2 + a + 1],
                                    None, ALU.mult))
                            assert c_v.n == v_ytm[g]
                        assert c_v.n == v_end
                    blk.vector(_m_v)

                    def _m_a(sc):
                        for g in range(NGR):
                            for fi in range(32):
                                if fi == 0:
                                    sc.wait_ge(c_v.sem, v_bias[g])
                                sc.wait_ge(c_pe.sem, pe_p1[(g, fi)])
                                bcol = (fi % 4) * 8 + fi // 4
                                c_a.inc(sc.activation(
                                    hv[:, fi, :], PB[fi % 2][:, 0:256],
                                    AF.Relu,
                                    bias=b1fm[:, bcol:bcol + 1]))
                                assert c_a.n == a_h3(g, fi)
                        assert c_a.n == a_end
                    blk.scalar(_m_a)

            if debug:
                with nc.Block(no_gpsimd_drain=True) as blk:
                    def _dbg2(gp9):
                        for nm, src in [("cnts", cnts[:]), ("offs", offs[:]),
                                        ("grpexp", grpexp[:]),
                                        ("sid", d_sid),
                                        ("gat", d_gat[0:4096, :]),
                                        ("xtm", d_xtm[0:4096, :]),
                                        ("ytm", d_ytm[0:4096, :])]:
                            if nm in dbg:
                                c_dbg.inc(gp9.dma_start(dbg[nm], src), 16)
                        gp9.wait_ge(c_dbg.sem, c_dbg.n)
                    blk.gpsimd(_dbg2)

            if scut >= 4 and p2stage >= 4:
                _w1f = w1g[:].rearrange("p k f -> p (k f)")
                yS = [_w1f[:, 0:4096], _w1f[:, 4096:8192]]
                xmS = [_w1f[:, 8192:12288], _w1f[:, 12288:16384]]
                sqS = _w1f[:, 16384:20480]
                oS = [hbuf[:].bitcast(F32),
                      _w1f[:, 20480:28672].bitcast(F32)]

                def v3(ap):
                    return ap.rearrange("p (a d) -> p a d", a=4)
                mu4 = muf[:, 0:4]; sm4 = muf[:, 4:8]
                var4 = varf[:, 0:4]; rst4 = rstdf[:, 0:4]
                t4a = s2f[:, 0:4]; t4b = s2f[:, 4:8]
                xl = {}; gl = {}; so = {}; vdone = {}; sqm = {}
                NSB = 8

                def yview(S):
                    return d_ytm[S * 512:(S + 1) * 512, :].rearrange(
                        "(a p) d -> p a d", p=128)

                def xmview(S):
                    return d_xmt[S * 512:(S + 1) * 512, :].rearrange(
                        "(a p) d -> p a d", p=128)

                with nc.Block(no_gpsimd_drain=True) as blk:
                    def _q0(sy):
                        for chn in c_scl:
                            sy.wait_ge(chn.sem, chn.n)
                        for b in range(2):
                            xl[b] = c_x.inc(
                                sy.dma_start(v3(yS[b]), yview(b)), 16)
                    blk.sync(_q0)

                    def _q0g(gp):
                        for chn in c_scl:
                            gp.wait_ge(chn.sem, chn.n)
                        for b in range(2):
                            gl[b] = c_g.inc(
                                gp.dma_start(v3(xmS[b]), xmview(b)), 16)
                    blk.gpsimd(_q0g)
                for b in range(NSB):
                    sl = b % 2
                    with nc.Block(no_gpsimd_drain=True) as blk:
                        def _c1(sc, b=b, sl=sl):
                            sc.wait_ge(c_x.sem, xl[b])
                            if b >= 1:
                                sc.wait_ge(c_v.sem, sqm[b - 1])
                            c_a.inc(sc.activation(sqS, yS[sl], AF.Square))
                        blk.scalar(_c1)
                        amark = c_a.n

                        def _c2(v, b=b, sl=sl, amark=amark):
                            def e(inst):
                                c_v.inc(inst); c_v.wait(v)
                            v.wait_ge(c_a.sem, amark)
                            e(v.tensor_reduce(sm4, v3(sqS), AX.X, ALU.add))
                            sqm[b] = c_v.n
                            e(v.tensor_reduce(mu4, v3(yS[sl]), AX.X,
                                              ALU.add))
                            e(v.tensor_scalar(mu4, mu4, 1.0 / D, None,
                                              ALU.mult))
                            e(v.tensor_mul(var4, mu4, mu4))
                            e(v.tensor_scalar(sm4, sm4, 1.0 / D, None,
                                              ALU.mult))
                            e(v.tensor_sub(var4, sm4, var4))
                            e(v.tensor_scalar(var4, var4, 1.0, 1e-5,
                                              ALU.mult, ALU.add))
                            _rsqrt(v, c_v, rst4, var4, t4a, t4b)
                            if b >= 2:
                                v.wait_ge(c_out.sem, so[b - 2])
                            mubc = mu4.unsqueeze(2).to_broadcast(
                                [128, 4, 1024])
                            rstbc = rst4.unsqueeze(2).to_broadcast(
                                [128, 4, 1024])
                            e(v.tensor_sub(v3(oS[sl]), v3(yS[sl]), mubc))
                            e(v.tensor_mul(v3(oS[sl]), v3(oS[sl]), rstbc))
                            v.wait_ge(c_g.sem, gl[b])
                            c_v.inc(v.tensor_add(v3(oS[sl]), v3(oS[sl]),
                                                 v3(xmS[sl])))
                            vdone[b] = c_v.n
                        blk.vector(_c2)

                        def _c3(sy, b=b, sl=sl):
                            sy.wait_ge(c_v.sem, vdone[b])
                            so[b] = c_out.inc(sy.dma_start(
                                d_out[b * 512:(b + 1) * 512, :].rearrange(
                                    "(a p) d -> p a d", p=128),
                                v3(oS[sl])), 16)
                            if b + 2 < NSB:
                                nb = b + 2
                                xl[nb] = c_x.inc(sy.dma_start(
                                    v3(yS[sl]), yview(nb)), 16)
                        blk.sync(_c3)

                        def _c3g(gp, b=b, sl=sl):
                            if b + 2 < NSB:
                                gp.wait_ge(c_v.sem, vdone[b])
                                nb = b + 2
                                gl[nb] = c_g.inc(gp.dma_start(
                                    v3(xmS[sl]), xmview(nb)), 16)
                        blk.gpsimd(_c3g)

        with nc.Block(no_gpsimd_drain=True) as blk:
            def _fin(sy):
                sy.wait_ge(c_out.sem, c_out.n)
            blk.sync(_fin)

    return nc


def prep_inputs(inputs):
    f = np.float32
    seq = np.asarray(inputs["L0_attn_seq"], f)
    wkv = np.ascontiguousarray(np.asarray(inputs["L0_attn_w_kv"], f))
    wq = np.ascontiguousarray(np.asarray(inputs["L0_attn_w_q"], f))
    wcomb = np.ascontiguousarray(np.asarray(inputs["L0_attn_w_combine"], f))
    wsg = np.ascontiguousarray(np.concatenate(
        [np.asarray(inputs["L0_attn_w_step"], f),
         np.asarray(inputs["L0_attn_w_gate"], f)], axis=1))
    wmd = np.ascontiguousarray(np.concatenate(
        [np.asarray(inputs["L0_attn_w_mom"], f),
         np.asarray(inputs["L0_attn_w_decay"], f)], axis=1))
    smg = np.ascontiguousarray(
        np.asarray(inputs["L0_attn_store_norm_g"], f).reshape(KC, 128).T)
    rmg = np.ascontiguousarray(
        np.asarray(inputs["L0_attn_retrieve_norm_g"], f).reshape(KC, 128).T)
    mw0 = np.ascontiguousarray(np.asarray(inputs["L0_attn_mem_w0"], f))
    mw1 = np.ascontiguousarray(np.asarray(inputs["L0_attn_mem_w1"], f))
    mhg = np.ascontiguousarray(
        np.asarray(inputs["L0_attn_mh_gamma"], f)[:, 0, :].T)
    rw = np.ascontiguousarray(np.asarray(inputs["L0_ffn_router_W"], f))
    rb = np.ascontiguousarray(
        np.asarray(inputs["L0_ffn_router_b"], f).reshape(1, NE))
    w1e = np.ascontiguousarray(np.asarray(inputs["L0_ffn_W1"], f))
    w2e = np.ascontiguousarray(np.asarray(inputs["L0_ffn_W2"], f))
    b1 = np.asarray(inputs["L0_ffn_b1"], f)
    b2 = np.asarray(inputs["L0_ffn_b2"], f)
    b1t = np.ascontiguousarray(
        b1.reshape(NE, 32, 128).transpose(2, 0, 1).reshape(128, NE * 32))
    b2t = np.ascontiguousarray(
        b2.reshape(NE, 8, 128).transpose(2, 0, 1).reshape(128, NE * 8))
    b1r = np.ascontiguousarray(b1.reshape(NE, DFF))
    b2r = np.ascontiguousarray(b2.reshape(NE, D))
    tri = np.triu(np.ones((128, 128), f), 1)
    tri8 = np.triu(np.ones((8, 8), f), 1)
    tokid = np.ascontiguousarray(
        (np.arange(32)[None, :] * 128 +
         np.arange(128)[:, None]).astype(np.int32))
    dcol = (np.arange(8)[None, :] * 128 +
            np.arange(128)[:, None]).astype(f)
    slot = np.broadcast_to(np.arange(24, dtype=f)[None, :] * 256.0,
                           (8, 24)).copy()

    ones = np.ones((128, 128), f)
    ident = np.eye(128, dtype=f)
    oh8 = np.zeros((8, 8 * 128), f)
    for h in range(8):
        oh8[h, h * 128:(h + 1) * 128] = 1.0
    ohc8 = np.zeros((128, 64), f)
    for h in range(8):
        ohc8[:, h * 8 + h] = 1.0 / 128.0
    bd8 = np.zeros((8, 64), f)
    for k in range(8):
        for c in range(8):
            bd8[k, c * 8 + k] = 1.0
    shared = dict(wkv=wkv, wq=wq, wcomb=wcomb, wsg=wsg, wmd=wmd, smg=smg,
                  rmg=rmg, mw0=mw0, mw1=mw1, mhg=mhg, rw=rw, rb=rb,
                  w1e=w1e, w2e=w2e, b1r=b1r, b2r=b2r,
                  c_tri=tri, c_tri8=tri8, c_tokid=tokid, c_dcol=dcol,
                  c_slot=slot,
                  c_ones=ones, c_id=ident, c_oh8=oh8, c_ohc8=ohc8,
                  c_bd8=bd8)
    in_maps = []
    for c in range(8):
        s, hf = divmod(c, 2)
        seq_fm = np.ascontiguousarray(seq[s].T)
        xwin = np.ascontiguousarray(seq[s, hf * 4096:(hf + 1) * 4096, :].T)
        gidx = np.zeros((128, KC), np.int32)
        for kc in range(KC):
            gidx[:, kc] = 2 * (kc * 128 + np.arange(128)) + hf
        m = dict(shared)
        m.update(seq_fm=seq_fm, xwin_fm=xwin, gidx=gidx)
        in_maps.append(m)
    return in_maps


def kernel(**inputs):
    in_maps = prep_inputs(inputs)
    nc = build_nc()
    res = run_bass_kernel_spmd(nc, in_maps, core_ids=list(range(8)))
    out = np.empty((4, N, D), np.float32)
    for c in range(8):
        s, hf = divmod(c, 2)
        out[s, hf * 4096:(hf + 1) * 4096, :] = \
            np.asarray(res.results[c]["out"])
    return out
